# version 22
# baseline (speedup 1.0000x reference)
import numpy as np
import ml_dtypes
from contextlib import ExitStack

import concourse.bass as bass
import concourse.mybir as mybir
from concourse.bass_utils import run_bass_kernel_spmd

F32 = mybir.dt.float32
BF16 = mybir.dt.bfloat16
AF = mybir.ActivationFunctionType
ALU = mybir.AluOpType
bf16 = ml_dtypes.bfloat16

D = 2048
H = 16
HS = 128
NT = 16
DC = 16
EPS = 1e-5
SCALE = float(1.0 / np.sqrt(np.float32(HS)))

ENGINES = ("sync", "gpsimd", "tensor", "vector", "scalar")
DEBUG = False


class Plan:

    def __init__(self, sems):
        self.ops = {e: [] for e in ENGINES}
        self.sems = sems
        self.last_tok = sems.setdefault("__last_tok", {})

    def n(self, name):
        return self.sems[name][1]

    def op(self, eng, fn, waits=(), inc=None, inc_by=1, is_dma=False):
        if inc is not None:
            self.sems[inc][1] += inc_by
        self.ops[eng].append((fn, tuple(waits), inc, inc_by, is_dma))
        return (inc, self.sems[inc][1]) if inc is not None else None

    def dma(self, eng, out, in_, inc, waits=(), **kw):
        def fn(e, o=out, i=in_, kw=kw):
            return e.dma_start(out=o, in_=i, **kw)
        return self.op(eng, fn, waits, inc, 16, is_dma=True)

    def barrier(self, eng, waits):
        self.ops[eng].append((None, tuple(waits), None, 0, False))

    def emit(self, nc):
        with nc.Block() as block:
            for eng in ENGINES:
                ops = self.ops[eng]
                if not ops:
                    continue
                selfs = ("__self_" + eng) if eng in ("vector", "scalar") else None

                def body(e, _ops=ops, _selfs=selfs, _eng=eng):
                    sh = self.sems[_selfs] if _selfs else None
                    last = self.last_tok.get(_eng)
                    for fn, waits, inc, inc_by, is_dma in _ops:
                        for (sname, v) in waits:
                            e.wait_ge(self.sems[sname][0], v)
                        if fn is None:
                            continue
                        if sh is not None and last is not None and not is_dma:
                            e.wait_ge(last[0], last[1])
                        ins = fn(e)
                        if inc is not None:
                            s = self.sems[inc]
                            s[2] += inc_by
                            ins.then_inc(s[0], inc_by)
                            if sh is not None and not is_dma:
                                last = (s[0], s[2])
                        elif sh is not None and not is_dma:
                            sh[2] += 1
                            ins.then_inc(sh[0], 1)
                            last = (sh[0], sh[2])
                    if _selfs:
                        self.last_tok[_eng] = last
                getattr(block, eng)(body)


def _mk_sems(ctx, nc, names):
    names = list(names) + ["__self_vector", "__self_scalar"]
    return {n: [ctx.enter_context(nc.semaphore(n)), 0, 0] for n in names}


def _ident(plan, ident, sem):
    plan.op("gpsimd", lambda e: e.memset(ident[:, :], 0.0), inc=sem)
    plan.op("gpsimd", lambda e: e.affine_select(
        out=ident[:, :], in_=ident[:, :], compare_op=ALU.not_equal,
        fill=1.0, base=0, pattern=[[-1, 128]], channel_multiplier=1),
        inc=sem)



def build_l1():
    nc = bass.Bass("TRN2", target_bir_lowering=False, debug=False)
    x = nc.dram_tensor("x", [2048, D], F32, kind="ExternalInput").ap()
    wq = nc.dram_tensor("wq", [D, D], F32, kind="ExternalInput").ap()
    bq = nc.dram_tensor("bq", [D], F32, kind="ExternalInput").ap()
    wk = nc.dram_tensor("wk", [D, D], F32, kind="ExternalInput").ap()
    bk = nc.dram_tensor("bk", [D], F32, kind="ExternalInput").ap()
    wv = nc.dram_tensor("wv", [D, D], F32, kind="ExternalInput").ap()
    bv = nc.dram_tensor("bv", [D], F32, kind="ExternalInput").ap()
    w1 = nc.dram_tensor("w1", [D, D], F32, kind="ExternalInput").ap()
    w3 = nc.dram_tensor("w3", [D, D], F32, kind="ExternalInput").ap()
    w2 = nc.dram_tensor("w2", [D, D], F32, kind="ExternalInput").ap()
    wr = nc.dram_tensor("wr", [8, D], F32, kind="ExternalInput").ap()
    KT_o = nc.dram_tensor("KT", [D, 2048], BF16, kind="ExternalOutput").ap()
    V_o = nc.dram_tensor("V", [2048, D], BF16, kind="ExternalOutput").ap()
    QT_o = nc.dram_tensor("QT", [D, 2048], BF16, kind="ExternalOutput").ap()
    p_o = nc.dram_tensor("p", [8, D], F32, kind="ExternalOutput").ap()
    if DEBUG:
        dbg_xnT = nc.dram_tensor("dbg_xnT", [128, DC * 2048], BF16,
                                 kind="ExternalOutput").ap()
        dbg_W = nc.dram_tensor("dbg_W", [128, 128], BF16,
                               kind="ExternalOutput").ap()
        dbg_h1 = nc.dram_tensor("dbg_h1", [128, 2048], BF16,
                                kind="ExternalOutput").ap()
        dbg_h2 = nc.dram_tensor("dbg_h2", [128, 2048], BF16,
                                kind="ExternalOutput").ap()

    with ExitStack() as ctx:
        E = ctx.enter_context
        xnT = E(nc.sbuf_tensor("xnT", [128, DC * 2048], BF16))
        h1T = E(nc.sbuf_tensor("h1T", [128, 16 * 2048], BF16))
        wslab = [E(nc.sbuf_tensor(f"wslab{i}", [128, 2048], BF16)) for i in range(2)]
        wtsl = [E(nc.sbuf_tensor(f"wtsl{i}", [128, 2048], BF16)) for i in range(2)]
        stage = [E(nc.sbuf_tensor(f"stage{i}", [128, 2048], BF16)) for i in range(2)]
        h3t = [E(nc.sbuf_tensor(f"h3t{i}", [128, 512], BF16)) for i in range(2)]
        xt = [E(nc.sbuf_tensor("xt0", [128, 2048], F32)), None]
        xnb = [E(nc.sbuf_tensor(f"xnb{i}", [128, 2048], BF16)) for i in range(2)]
        xcen = E(nc.sbuf_tensor("xcen", [128, 2048], F32))
        ident = E(nc.sbuf_tensor("ident", [128, 128], BF16))
        eps_t = E(nc.sbuf_tensor("eps_t", [128, 1], F32))
        bqb = E(nc.sbuf_tensor("bqb", [128, 16], F32))
        bkb = E(nc.sbuf_tensor("bkb", [128, 16], F32))
        Bv = E(nc.sbuf_tensor("Bv", [128, 2048], F32))
        xt[1] = Bv
        ones1 = E(nc.sbuf_tensor("ones1", [1, 128], F32))
        wr_bf = E(nc.sbuf_tensor("wr_bf", [8, 2048], BF16))
        wrT = E(nc.sbuf_tensor("wrT", [128, 16 * 8], BF16))
        Wsb = E(nc.sbuf_tensor("Wsb", [128, 16 * 8], BF16))
        out8 = E(nc.sbuf_tensor("out8", [8, 2048], F32))
        bvrow = out8[0:1, :]
        s1 = [E(nc.sbuf_tensor(f"s1_{i}", [128, 1], F32)) for i in range(2)]
        mmean = [E(nc.sbuf_tensor(f"m_{i}", [128, 1], F32)) for i in range(2)]
        m2t = [E(nc.sbuf_tensor(f"m2_{i}", [128, 1], F32)) for i in range(2)]
        vvar = [E(nc.sbuf_tensor(f"vv_{i}", [128, 1], F32)) for i in range(2)]
        sacc = [E(nc.sbuf_tensor(f"sacc_{i}", [128, 1], F32)) for i in range(2)]
        rstd = [E(nc.sbuf_tensor(f"rstd_{i}", [128, 1], F32)) for i in range(2)]
        r_eq1 = E(nc.sbuf_tensor("r_eq1", [128, 8], F32))
        r_eq2 = E(nc.sbuf_tensor("r_eq2", [128, 8], F32))
        r_l2 = E(nc.sbuf_tensor("r_l2", [128, 8], F32))
        r_t1 = E(nc.sbuf_tensor("r_t1", [128, 8], F32))
        r_m1 = E(nc.sbuf_tensor("r_m1", [128, 1], F32))
        r_m2 = E(nc.sbuf_tensor("r_m2", [128, 1], F32))
        r_d = E(nc.sbuf_tensor("r_d", [128, 1], F32))
        r_g1 = E(nc.sbuf_tensor("r_g1", [128, 1], F32))
        r_g2 = E(nc.sbuf_tensor("r_g2", [128, 1], F32))

        sems = _mk_sems(ctx, nc, [
            "sX", "eSq", "vStat", "eRstd", "vXN", "vMn", "eXc", "tTr",
            "vTr", "gMisc",
            "sB", "gK", "tK", "eK", "sKout", "gV", "tV", "vV", "sV",
            "tBv", "vBv", "gWr", "tWr", "vWrT", "tR", "vRm", "eRg", "vRt",
            "gF", "tFtr", "vFtr", "tH", "eH", "vH3", "tS", "vO8", "sP",
        ])

        with ExitStack() as c2:
            psA = [c2.enter_context(nc.psum_tensor(f"psA{i}", [128, 128], BF16))
                   for i in range(2)]
            psB = [c2.enter_context(nc.psum_tensor(f"psB{i}", [128, 512], F32))
                   for i in range(4)]
            psR = c2.enter_context(nc.psum_tensor("psR", [128, 8], F32))
            psTB = psA
            plan = Plan(sems)
            _ident(plan, ident, "gMisc")
            plan.op("gpsimd", lambda e: e.memset(ones1[:, :], 1.0), inc="gMisc")
            plan.op("gpsimd", lambda e: e.memset(eps_t[:, :], EPS), inc="gMisc")
            for (bsb, bsrc) in ((bkb, bk), (bqb, bq)):
                for c in range(16):
                    plan.dma("scalar", bsb[:, c:c + 1],
                             bsrc[c * 128:(c + 1) * 128].rearrange(
                                 "(p o) -> p o", o=1), inc="sB")
            plan.dma("scalar", bvrow,
                     bv.rearrange("(o f) -> o f", o=1), inc="sB")

            gtr = 0
            for tt in range(NT):
                p = tt % 2
                waits = [("eSq", tt - 1),
                         ("vXN", tt - 1)] if tt >= 2 else []
                dx = plan.dma("sync" if tt % 2 == 0 else "scalar",
                              xt[p][:, :],
                              x[tt * 128:(tt + 1) * 128, :], inc="sX",
                              waits=waits)
                plan.op("vector", lambda e, p=p: e.reduce_sum(
                    s1[p][:, :], xt[p][:, :], axis=mybir.AxisListType.X),
                    waits=[dx])
                plan.op("vector", lambda e, p=p: e.tensor_scalar_mul(
                    mmean[p][:, :], s1[p][:, :], -1.0 / 2048.0), inc="vMn")
                sq_w = [dx] + ([("vStat", tt - 1)] if tt >= 2 else [])
                sq = plan.op("scalar", lambda e, p=p: e.activation(
                    stage[0][:, :], xt[p][:, :], AF.Square,
                    accum_out=sacc[p][:, :]), waits=sq_w, inc="eSq")
                plan.op("vector", lambda e, p=p: e.tensor_scalar_mul(
                    s1[p][:, :], sacc[p][:, :], 1.0 / 2048.0), waits=[sq])
                plan.op("vector", lambda e, p=p: e.tensor_mul(
                    m2t[p][:, :], mmean[p][:, :], mmean[p][:, :]))
                st = plan.op("vector", lambda e, p=p: e.tensor_sub(
                    vvar[p][:, :], s1[p][:, :], m2t[p][:, :]), inc="vStat")
                ln_w = [st] + ([("gMisc", 4)] if tt == 0 else [])
                plan.op("scalar", lambda e, p=p: e.activation(
                    m2t[p][:, :], vvar[p][:, :], AF.Ln, bias=eps_t[:, :]),
                    waits=ln_w)
                plan.op("scalar", lambda e, p=p: e.activation(
                    rstd[p][:, :], m2t[p][:, :], AF.Exp, scale=-0.5),
                    inc="eRstd")
                xc_w = [("vMn", tt + 1)]
                if tt >= 1:
                    xc_w.append(("vXN", tt))
                xc = plan.op("scalar", lambda e, p=p: e.activation(
                    xcen[:, :], xt[p][:, :], AF.Identity,
                    bias=mmean[p][:, :]), waits=xc_w, inc="eXc")
                xn_w = [xc, ("eRstd", tt + 1)] +                     ([("tTr", (tt - 2) * 16 + 16)] if tt >= 2 else [])
                plan.op("vector", lambda e, p=p: e.tensor_scalar_mul(
                    xnb[p][:, :], xcen[:, :], rstd[p][:, :]),
                    waits=xn_w, inc="vXN")
                for dc in range(DC):
                    g = gtr
                    w = [("vXN", tt + 1), ("gMisc", 3)]
                    if g >= 2:
                        w.append(("vTr", g - 1))
                    plan.op("tensor", lambda e, p=p, dc=dc, g=g: e.transpose(
                        psA[g % 2][:, :], xnb[p][:, dc * 128:(dc + 1) * 128],
                        ident[:, :]), waits=w, inc="tTr")
                    plan.op("vector", lambda e, tt=tt, dc=dc, g=g: e.tensor_copy(
                        xnT[:, dc * 2048 + tt * 128: dc * 2048 + (tt + 1) * 128],
                        psA[g % 2][:, :]), waits=[("tTr", g + 1)], inc="vTr")
                    gtr += 1


            def kq_phase(wsrc, bsb, out_ap, mm_gate, cast_guard,
                         rdy=lambda fc, t4: []):
                base_tK = plan.n("tK")
                base_sK = plan.n("sKout")
                for fc in range(16):
                    if fc >= 2:
                        cw = [("tK", base_tK + 4 * (fc - 1))]
                    else:
                        cw = list(cast_guard)
                    src = wsrc.rearrange("(dc p) f -> p dc f", p=128)[
                        :, :, fc * 128:(fc + 1) * 128]
                    dst = wslab[fc % 2][:, :].rearrange(
                        "p (dc f) -> p dc f", f=128)
                    cast = plan.dma("gpsimd", dst, src, inc="gK", waits=cw)
                    for tp in range(2):
                        t4a, t4b = 2 * tp, 2 * tp + 1
                        gi = plan.n("tK")
                        pb = ((gi // 2) % 2) * 2
                        for dc in range(DC):
                            for sl, t4 in ((0, t4a), (1, t4b)):
                                w = []
                                if dc == 0 and sl == 0:
                                    w = [cast, ("eK", max(0, gi - 2))]
                                    if fc == 0 and tp == 0:
                                        w += mm_gate
                                    w += rdy(fc, t4b)
                                plan.op("tensor",
                                        lambda e, fc=fc, t4=t4, dc=dc,
                                        bk=pb + sl: e.matmul(
                                            psB[bk][:, :],
                                            wslab[fc % 2][:, dc * 128:
                                                          (dc + 1) * 128],
                                            xnT[:, dc * 2048 + t4 * 512:
                                                dc * 2048 + (t4 + 1) * 512],
                                            start=(dc == 0),
                                            stop=(dc == DC - 1)),
                                        waits=w,
                                        inc="tK" if dc == DC - 1 else None)
                        for off, t4 in ((1, t4a), (2, t4b)):
                            w = [("tK", gi + off)]
                            if fc == 0 and tp == 0 and off == 1 \
                                    and base_sK == 0:
                                w.append(("sB", 528))
                            if tp == 0 and off == 1:
                                if fc >= 2:
                                    w.append(("sKout", base_sK + 16 * (fc - 1)))
                                elif base_sK > 0:
                                    w.append(("sKout", base_sK))
                            plan.op("scalar",
                                    lambda e, fc=fc, t4=t4, bk=pb + off - 1,
                                    bsb=bsb: e.activation(
                                        stage[fc % 2][:, t4 * 512:
                                                      (t4 + 1) * 512],
                                        psB[bk][:, :], AF.Identity,
                                        bias=bsb[:, fc:fc + 1]),
                                    waits=w, inc="eK")
                    plan.dma("sync", out_ap[fc * 128:(fc + 1) * 128, :],
                             stage[fc % 2][:, :], inc="sKout",
                             waits=[("eK", plan.n("eK"))])

            if DEBUG:
                plan.dma("sync", dbg_xnT[:, :], xnT[:, :], inc="sP",
                         waits=[("vTr", 256)])
            kq_phase(wk, bkb, KT_o, [], [],
                     rdy=lambda fc, t4: (
                         [("vTr", (4 * t4 + 4) * 16)] if fc == 0 else []))

            for nn in range(4):
                w = [("gMisc", 3), ("sB", plan.n("sB")),
                     ("eK", plan.n("eK"))]
                if nn >= 2:
                    w.append(("vBv", nn - 1))
                plan.op("tensor", lambda e, nn=nn: e.matmul(
                    psB[nn % 2][:, :], ones1[:, :],
                    bvrow[:, nn * 512:(nn + 1) * 512], start=True, stop=True),
                    waits=w, inc="tBv")
                plan.op("vector", lambda e, nn=nn: e.tensor_copy(
                    Bv[:, nn * 512:(nn + 1) * 512], psB[nn % 2][:, :]),
                    waits=[("tBv", nn + 1)], inc="vBv")

            wvslab = [h1T[:, 0:16 * 512], h1T[:, 16 * 512:32 * 512]]
            for f4 in range(4):
                if f4 >= 2:
                    w = [("tV", 16 * (f4 - 1))]
                else:
                    w = [("vTr", 256)]
                src = wv.rearrange("(dc p) f -> p dc f", p=128)[
                    :, :, f4 * 512:(f4 + 1) * 512]
                dst = wvslab[f4 % 2].rearrange("p (dc f) -> p dc f", f=512)
                cast = plan.dma("gpsimd", dst, src, inc="gV", waits=w)
                for tt in range(NT):
                    gi = plan.n("tV")
                    par = gi % 2
                    for dc in range(DC):
                        w = []
                        if dc == 0:
                            w = [cast, ("vV", max(0, gi - 1))]
                            if gi == 0:
                                w.append(("eK", 64))
                        plan.op("tensor",
                                lambda e, f4=f4, tt=tt, dc=dc, par=par:
                                e.matmul(
                                    psB[par][:, :],
                                    xnT[:, dc * 2048 + tt * 128:
                                        dc * 2048 + (tt + 1) * 128],
                                    wvslab[f4 % 2][:, dc * 512:(dc + 1) * 512],
                                    start=(dc == 0), stop=(dc == DC - 1)),
                                waits=w, inc="tV" if dc == DC - 1 else None)
                    w = [("tV", gi + 1), ("vBv", 4)]
                    if gi >= 2:
                        w.append(("sV", 16 * (gi - 1)))
                    plan.op("vector", lambda e, f4=f4, tt=tt, par=par:
                            e.scalar_tensor_tensor(
                                h3t[gi % 2][:, :],
                                psB[par][:, :], 1.0,
                                Bv[:, f4 * 512:(f4 + 1) * 512],
                                op0=ALU.mult, op1=ALU.add),
                            waits=w, inc="vV")
                    plan.dma("sync",
                             V_o[tt * 128:(tt + 1) * 128,
                                 f4 * 512:(f4 + 1) * 512],
                             h3t[gi % 2][:, :], inc="sV",
                             waits=[("vV", plan.n("vV"))])

            kq_phase(wq, bqb, QT_o, [("vV", 64)], [("tK", 64)])

            plan.dma("gpsimd", wr_bf[:, :], wr[:, :], inc="gWr",
                     waits=[("vXN", 16)])
            for dc in range(DC):
                w = [("gWr", 16), ("gMisc", 3)]
                if dc == 0:
                    w.append(("eK", plan.n("eK")))
                if dc >= 2:
                    w.append(("vWrT", dc - 1))
                plan.op("tensor", lambda e, dc=dc: e.transpose(
                    psTB[dc % 2][:, 0:8], wr_bf[:, dc * 128:(dc + 1) * 128],
                    ident[0:8, 0:8]), waits=w, inc="tWr")
                plan.op("vector", lambda e, dc=dc: e.tensor_copy(
                    wrT[:, dc * 8:(dc + 1) * 8], psTB[dc % 2][:, 0:8]),
                    waits=[("tWr", dc + 1)], inc="vWrT")
            for tt in range(NT):
                for dc in range(DC):
                    w = []
                    if dc == 0:
                        w = [("vWrT", 16)]
                        if tt:
                            w.append(("vRt", tt))
                    plan.op("tensor", lambda e, tt=tt, dc=dc: e.matmul(
                        psR[:, :],
                        xnT[:, dc * 2048 + tt * 128: dc * 2048 + (tt + 1) * 128],
                        wrT[:, dc * 8:(dc + 1) * 8],
                        start=(dc == 0), stop=(dc == DC - 1)),
                        waits=w, inc="tR" if dc == DC - 1 else None)
                plan.op("vector", lambda e, tt=tt: e.reduce_max(
                    r_m1[:, :], psR[:, :], axis=mybir.AxisListType.X),
                    waits=[("tR", tt + 1)])
                plan.op("vector", lambda e: e.tensor_single_scalar(
                    r_eq1[:, :], psR[:, :], r_m1[:, :], ALU.is_equal))
                plan.op("vector", lambda e: e.scalar_tensor_tensor(
                    r_l2[:, :], r_eq1[:, :], -1e30, psR[:, :],
                    op0=ALU.mult, op1=ALU.add))
                plan.op("vector", lambda e: e.reduce_max(
                    r_m2[:, :], r_l2[:, :], axis=mybir.AxisListType.X))
                plan.op("vector", lambda e: e.tensor_single_scalar(
                    r_eq2[:, :], r_l2[:, :], r_m2[:, :], ALU.is_equal))
                plan.op("vector", lambda e: e.tensor_sub(
                    r_d[:, :], r_m1[:, :], r_m2[:, :]), inc="vRm")
                plan.op("scalar", lambda e, tt=tt: e.activation(
                    r_g1[:, :], r_d[:, :], AF.Sigmoid),
                    waits=[("vRm", tt + 1)])
                plan.op("scalar", lambda e: e.activation(
                    r_g2[:, :], r_d[:, :], AF.Sigmoid, scale=-1.0), inc="eRg")
                plan.op("vector", lambda e: e.tensor_scalar_mul(
                    r_t1[:, :], r_eq1[:, :], r_g1[:, :]),
                    waits=[("eRg", tt + 1)])
                plan.op("vector", lambda e, tt=tt: e.scalar_tensor_tensor(
                    Wsb[:, tt * 8:(tt + 1) * 8], r_eq2[:, :], r_g2[:, :],
                    r_t1[:, :], op0=ALU.mult, op1=ALU.add), inc="vRt")
            plan.emit(nc)

        with ExitStack() as c2:
            psC = [c2.enter_context(nc.psum_tensor(f"psC{i}", [128, 512], F32))
                   for i in range(4)]
            psT = [c2.enter_context(nc.psum_tensor(f"psT{i}", [128, 128], BF16))
                   for i in range(2)]
            plan = Plan(sems)

            def hstage(wsrc, in_sb, out_sb, act, mm_gate, cast_guard):
                base_tFtr = plan.n("tFtr")
                base_tH = plan.n("tH")
                for ic in range(16):
                    if ic >= 2:
                        cw = [("tFtr", base_tFtr + 16 * (ic - 1))]
                    else:
                        cw = list(cast_guard)
                    cast = plan.dma("gpsimd", wslab[ic % 2][:, :],
                                    wsrc[ic * 128:(ic + 1) * 128, :],
                                    inc="gF", waits=cw)
                    for dc in range(DC):
                        g = plan.n("tFtr")
                        w = [cast]
                        if ic == 0 and dc == 0:
                            w += mm_gate
                        if g >= 2:
                            w.append(("vFtr", g - 1))
                        plan.op("tensor", lambda e, ic=ic, dc=dc, g=g:
                                e.transpose(
                                    psT[g % 2][:, :],
                                    wslab[ic % 2][:, dc * 128:(dc + 1) * 128],
                                    ident[:, :]), waits=w, inc="tFtr")
                        evw = [("tFtr", g + 1)]
                        if dc == 0:
                            if ic >= 2:
                                evw.append(("tH", base_tH + 4 * (ic - 1)))
                            elif base_tH > 0:
                                evw.append(("tH", base_tH))
                        plan.op("vector", lambda e, ic=ic, dc=dc, g=g:
                                e.tensor_copy(
                                    wtsl[ic % 2][:, dc * 128:(dc + 1) * 128],
                                    psT[g % 2][:, :]), waits=evw, inc="vFtr")
                    for tp in range(2):
                        t4a, t4b = 2 * tp, 2 * tp + 1
                        gi = plan.n("tH")
                        pb = ((gi // 2) % 2) * 2
                        for dc in range(DC):
                            for sl, t4 in ((0, t4a), (1, t4b)):
                                w = []
                                if dc == 0 and sl == 0:
                                    w = [("vFtr", plan.n("vFtr")),
                                         ("eH", max(0, gi - 2))]
                                plan.op("tensor",
                                        lambda e, ic=ic, t4=t4, dc=dc,
                                        bk=pb + sl: e.matmul(
                                            psC[bk][:, :],
                                            wtsl[ic % 2][:, dc * 128:
                                                         (dc + 1) * 128],
                                            in_sb[:, dc * 2048 + t4 * 512:
                                                  dc * 2048 + (t4 + 1) * 512],
                                            start=(dc == 0),
                                            stop=(dc == DC - 1)),
                                        waits=w,
                                        inc="tH" if dc == DC - 1 else None)
                        for off, t4 in ((1, t4a), (2, t4b)):
                            plan.op("scalar",
                                    lambda e, ic=ic, t4=t4, bk=pb + off - 1:
                                    e.activation(
                                        out_sb[:, ic * 2048 + t4 * 512:
                                               ic * 2048 + (t4 + 1) * 512],
                                        psC[bk][:, :], act),
                                    waits=[("tH", gi + off)], inc="eH")

            if DEBUG:
                plan.dma("sync", dbg_W[:, :], Wsb[:, :], inc="sP",
                         waits=[("vRt", 16)])
            gateC = [("eK", plan.n("eK")), ("vV", plan.n("vV")),
                     ("vRt", 16), ("sV", plan.n("sV"))]
            hstage(w1, xnT, h1T, AF.Silu, gateC, [("tK", 128)])
            base_h2_eH = plan.n("eH")
            hstage(w3, h1T, xnT, AF.Silu,
                   [("tH", plan.n("tH")), ("tR", 16)], [("tFtr", 256)])

            if DEBUG:
                plan.dma("sync", dbg_h1[:, :], h1T[:, 0:2048], inc="sP",
                         waits=[("eH", 64)])
                plan.dma("sync", dbg_h2[:, :], xnT[:, 0:2048], inc="sP",
                         waits=[("eH", 128)])
            plan.emit(nc)

        with ExitStack() as c3:
            psC = [c3.enter_context(nc.psum_tensor(f"psC3{i}", [128, 512],
                                                   F32)) for i in range(2)]
            psT = [c3.enter_context(nc.psum_tensor(f"psT3{i}", [128, 128],
                                                   BF16)) for i in range(2)]
            ps8 = c3.enter_context(nc.psum_tensor("ps8", [8, 2048], F32))
            plan = Plan(sems)
            h2T = xnT
            w2Tsl = [h1T[:, 0:16 * 512], h1T[:, 16 * 512:32 * 512]]
            base3_tH = plan.n("tH")
            base3_tFtr = plan.n("tFtr")
            eH_all = plan.n("eH")
            first_scatter = True
            for f4 in range(4):
                for k in range(4):
                    si3 = f4 * 4 + k
                    if si3 >= 2:
                        cw = [("tFtr", base3_tFtr + 16 * (si3 - 1))]
                    else:
                        cw = [("tFtr", base3_tFtr)]
                    cast = plan.dma("gpsimd", wslab[k % 2][:, :],
                                    w2[f4 * 512 + k * 128:
                                       f4 * 512 + (k + 1) * 128, :],
                                    inc="gF", waits=cw)
                    for ic in range(16):
                        g = plan.n("tFtr")
                        w = [cast]
                        if g == base3_tFtr:
                            w.append(("eH", eH_all))
                        if g >= 2:
                            w.append(("vFtr", g - 1))
                        plan.op("tensor", lambda e, k=k, ic=ic, g=g:
                                e.transpose(
                                    psT[g % 2][:, :],
                                    wslab[k % 2][:, ic * 128:(ic + 1) * 128],
                                    ident[:, :]), waits=w, inc="tFtr")
                        evw = [("tFtr", g + 1)]
                        if si3 == 0 and ic == 0:
                            evw.append(("tH", base3_tH))
                        if si3 >= 8 and ic == 0:
                            evw.append(("tH", base3_tH + (f4 - 1) * 16))
                        plan.op("vector", lambda e, f4=f4, k=k, ic=ic, g=g:
                                e.tensor_copy(
                                    w2Tsl[f4 % 2][:, ic * 512 + k * 128:
                                                  ic * 512 + (k + 1) * 128],
                                    psT[g % 2][:, :]), waits=evw, inc="vFtr")
                for tt in range(NT):
                    gl = f4 * 16 + tt
                    par = gl % 2
                    for ic in range(16):
                        w = []
                        if ic == 0:
                            w = [("vFtr", plan.n("vFtr"))]
                            if gl < 2:
                                w.append(("eH", eH_all))
                            else:
                                w.append(("vH3", gl - 1))
                        plan.op("tensor",
                                lambda e, f4=f4, tt=tt, ic=ic, par=par:
                                e.matmul(
                                    psC[par][:, :],
                                    h2T[:, ic * 2048 + tt * 128:
                                        ic * 2048 + (tt + 1) * 128],
                                    w2Tsl[f4 % 2][:, ic * 512:(ic + 1) * 512],
                                    start=(ic == 0), stop=(ic == 15)),
                                waits=w, inc="tH" if ic == 15 else None)
                    if tt > 0:
                        w = [("vH3", plan.n("vH3"))]
                        if first_scatter:
                            w.append(("vRt", 16))
                            first_scatter = False
                        plan.op("tensor", lambda e, f4=f4, tt=tt, gl=gl:
                                e.matmul(
                                    ps8[0:8, f4 * 512:(f4 + 1) * 512],
                                    Wsb[:, (tt - 1) * 8: tt * 8],
                                    h3t[(gl - 1) % 2][:, :],
                                    start=(tt - 1 == 0), stop=False),
                                waits=w, inc="tS")
                    w = [("tH", base3_tH + gl + 1)]
                    if gl >= 2:
                        w.append(("tS", plan.n("tS")))
                    plan.op("vector", lambda e, par=par: e.tensor_copy(
                        h3t[par][:, :], psC[par][:, :]), waits=w, inc="vH3")
                plan.op("tensor", lambda e, f4=f4: e.matmul(
                    ps8[0:8, f4 * 512:(f4 + 1) * 512],
                    Wsb[:, 15 * 8:16 * 8],
                    h3t[(f4 * 16 + 15) % 2][:, :],
                    start=False, stop=True),
                    waits=[("vH3", plan.n("vH3"))], inc="tS")
            plan.op("vector", lambda e: e.tensor_copy(out8[:, :], ps8[0:8, :]),
                    waits=[("tS", plan.n("tS"))], inc="vO8")
            plan.dma("sync", p_o[:, :], out8[:, :], inc="sP",
                     waits=[("vO8", 1)])
            plan.barrier("sync", [("sKout", plan.n("sKout")),
                                  ("sV", plan.n("sV")), ("sP", plan.n("sP"))])
            plan.emit(nc)
    return nc



def build_l2():
    nc = bass.Bass("TRN2", target_bir_lowering=False, debug=False)
    KTg = nc.dram_tensor("KTg", [4 * D, 2048], BF16, kind="ExternalInput").ap()
    Vg = nc.dram_tensor("Vg", [4 * 2048, D], BF16, kind="ExternalInput").ap()
    QT = nc.dram_tensor("QT", [D, 2048], BF16, kind="ExternalInput").ap()
    x = nc.dram_tensor("x", [2048, D], F32, kind="ExternalInput").ap()
    wo = nc.dram_tensor("wo", [D, D], F32, kind="ExternalInput").ap()
    bo = nc.dram_tensor("bo", [D], F32, kind="ExternalInput").ap()
    masks = nc.dram_tensor("masks", [4, 128, 128], BF16,
                           kind="ExternalInput").ap()
    sel = nc.dram_tensor("sel", [8, 2], F32, kind="ExternalInput").ap()
    pall = nc.dram_tensor("pall", [64, D], F32, kind="ExternalInput").ap()
    o = nc.dram_tensor("o", [2048, D], F32, kind="ExternalOutput").ap()

    with ExitStack() as ctx:
        E = ctx.enter_context
        QTs = E(nc.sbuf_tensor("QTs", [128, H * 2048], BF16))
        attnT = E(nc.sbuf_tensor("attnT", [128, H * 2048], BF16))
        Krs = [E(nc.sbuf_tensor(f"Krs{i}", [128, 2048], BF16)) for i in range(3)]
        Vrs = [E(nc.sbuf_tensor(f"Vrs{i}", [128, 2048], BF16)) for i in range(3)]
        pT = [E(nc.sbuf_tensor(f"pT{i}", [128, 1024], BF16)) for i in range(4)]
        mask_sb = E(nc.sbuf_tensor("mask_sb", [128, 4 * 128], BF16))
        xt = [E(nc.sbuf_tensor(f"xtL2{i}", [128, 512], F32)) for i in range(2)]
        Bo = E(nc.sbuf_tensor("Bo", [128, 2048], F32))
        outsb = [E(nc.sbuf_tensor(f"outsb{i}", [128, 512], F32))
                 for i in range(2)]
        ptmp = [E(nc.sbuf_tensor(f"ptmp{i}", [8, 2048], F32)) for i in range(2)]
        out8 = [E(nc.sbuf_tensor("out80", [8, 2048], F32))]
        s_row = None
        bc_sb = E(nc.sbuf_tensor("bc_sb", [128, 1024], F32))
        s_row = Bo[0:1, 1024:2048]
        lnv_sb = None
        ones1 = E(nc.sbuf_tensor("ones1L2", [1, 128], F32))
        ones_s = E(nc.sbuf_tensor("ones_s", [128, 1], BF16))
        sel_sb = E(nc.sbuf_tensor("sel_sb", [8, 2], F32))
        wos = QTs[:, 0:16 * 512]
        borow = out8[0][0:1, :]

        sems = _mk_sems(ctx, nc, [
            "gMisc", "sIn", "sKV", "tL", "nE", "nM", "tPV", "nEvac",
            "vRec", "tBc", "vBc", "gWo", "tO", "vOut", "sO", "tSel",
            "sX2", "sP2",
        ])

        with ExitStack() as c2:
            accT = c2.enter_context(nc.psum_tensor("accT", [128, 1024], F32))
            sums = c2.enter_context(nc.psum_tensor("sums", [128, 1024], F32))
            lg = [c2.enter_context(nc.psum_tensor(f"lg{i}", [128, 1024], F32))
                  for i in range(2)]
            plan = Plan(sems)
            plan.op("gpsimd", lambda e: e.memset(ones1[:, :], 1.0), inc="gMisc")
            plan.op("gpsimd", lambda e: e.memset(ones_s[:, :], 1.0), inc="gMisc")
            gm_all = plan.n("gMisc")
            plan.dma("sync", mask_sb[:, :].rearrange("p (r q) -> p r q", q=128),
                     masks.rearrange("r p q -> p r q"), inc="sIn")
            plan.dma("sync", sel_sb[:, :], sel[:, :], inc="sIn")
            plan.dma("sync", borow, bo.rearrange("(o f) -> o f", o=1),
                     inc="sIn")

            def qt_dma(hh):
                plan.dma("sync", QTs[:, hh * 2048:(hh + 1) * 2048],
                         QT[hh * 128:(hh + 1) * 128, :], inc="sIn")

            kv_guard = {}
            masked_cnt = {}
            state = {"cn": 0, "pend": None}

            def kv_dma(idx):
                h, rs = idx // 8, idx % 4
                pr = idx % 3
                w = [("gMisc", gm_all)] if idx < 3 else \
                    [("tPV", kv_guard[idx - 3])]
                plan.dma("sync", Krs[pr][:, :],
                         KTg[rs * 2048 + h * 128: rs * 2048 + (h + 1) * 128, :],
                         inc="sKV", waits=w)
                plan.dma("sync",
                         Vrs[pr][:, :].rearrange("p (k f) -> p k f", f=128),
                         Vg[rs * 2048:(rs + 1) * 2048,
                            h * 128:(h + 1) * 128].rearrange(
                             "(k s) f -> s k f", s=128),
                         inc="sKV", waits=w)

            def emit_pv(job):
                (h, half, rs, jj, mychunk, q0, cw, pr, first_of_half,
                 last_jj) = job
                hs = half * 1024
                masked = mychunk in masked_cnt
                pieces = []
                p0 = q0 + (128 if masked else 0)
                while p0 < q0 + cw:
                    pb = hs + ((p0 - hs) // 512 + 1) * 512
                    pieces.append((p0, min(pb, q0 + cw) - p0, False))
                    p0 = pb
                if masked:
                    pieces.append((q0, 128, True))
                if not pieces:
                    pieces = [(q0, cw, masked)]
                npc = len(pieces)
                for i, (pq, pw, is_masked) in enumerate(pieces):
                    bank = (pq - hs) // 512
                    w = []
                    if i == 0:
                        if first_of_half:
                            idx = h * 2 + half
                            if idx >= 1:
                                w.append(("nEvac", idx))
                        if not is_masked:
                            w.append(("nE", mychunk + 1))
                    if is_masked:
                        w.append(("nM", masked_cnt[mychunk]))
                    st_pv = (bank not in bank_started[0])
                    bank_started[0].add(bank)
                    plan.op("tensor",
                            lambda e, jj=jj, pq=pq, pw=pw, pr=pr, hs=hs,
                            q0=q0, mychunk=mychunk, st=st_pv: e.matmul(
                                accT[:, pq - hs: pq - hs + pw],
                                Vrs[pr][:, jj * 128:(jj + 1) * 128],
                                pT[mychunk % 4][:, pq - q0: pq - q0 + pw],
                                start=st, stop=False,
                                skip_group_check=True),
                            waits=w)
                    st_sm = (bank not in bank_started[1])
                    bank_started[1].add(bank)
                    plan.op("tensor",
                            lambda e, pq=pq, pw=pw, hs=hs, q0=q0,
                            mychunk=mychunk, st=st_sm: e.matmul(
                                sums[0:1, pq - hs: pq - hs + pw],
                                ones_s[:, :],
                                pT[mychunk % 4][:, pq - q0: pq - q0 + pw],
                                start=st, stop=False,
                                skip_group_check=True),
                            inc="tPV" if i == npc - 1 else None)

            def emit_epi(job):
                h, hs, hidx, vrec = job
                for k in range(2):
                    w = [("vRec", vrec)]
                    plan.op("tensor", lambda e, k=k: e.matmul(
                        sums[:, k * 512:(k + 1) * 512], ones1[:, :],
                        s_row[0:1, k * 512:(k + 1) * 512],
                        start=True, stop=True),
                        waits=w, inc="tBc")
                plan.op("scalar", lambda e: e.activation(
                    Bo[:, 0:1024], sums[:, :], AF.Ln),
                    waits=[("tBc", plan.n("tBc"))])
                plan.op("scalar", lambda e: e.activation(
                    bc_sb[:, :], Bo[:, 0:1024], AF.Exp, scale=-1.0),
                    inc="vBc")
                plan.op("vector", lambda e, h=h, hs=hs: e.tensor_mul(
                    attnT[:, h * 2048 + hs: h * 2048 + hs + 1024],
                    accT[:, :], bc_sb[:, :]),
                    waits=[("vBc", plan.n("vBc"))], inc="nEvac")

            bank_started = [set(), set()]
            qt_dma(0)
            qt_dma(1)
            for i in range(3):
                kv_dma(i)
            for hh in range(2, H):
                qt_dma(hh)
            tL_h = {}
            for h in range(H):
                for half in range(2):
                    hs = half * 1024
                    bank_started[0] = set()
                    bank_started[1] = set()
                    last_jj = {b: min(15, (hs + (b + 1) * 512 - 1) // 128)
                               for b in range(2)}
                    for rs in range(4):
                        it = (h * 2 + half) * 4 + rs
                        pr = it % 3
                        first_of_half = (rs == 0)
                        first_of_window = True
                        for jj in range(16):
                            q0g = max(hs, jj * 128)
                            qend = hs + 1024
                            if q0g >= qend:
                                continue
                            q0 = q0g
                            cw = qend - q0
                            mychunk = state["cn"]
                            for ls in range(0, cw, 512):
                                lw = min(512, cw - ls)
                                w = []
                                if ls == 0:
                                    if first_of_window:
                                        w.append(("sKV", 32 * (it + 1)))
                                    if it == (h * 2) * 4 and first_of_window:
                                        w.append(("sIn", 48 + 16 * (h + 1)))
                                    if mychunk >= 2:
                                        w.append(("nE", mychunk - 1))
                                plan.op("tensor",
                                        lambda e, h=h, jj=jj, q0=q0, ls=ls,
                                        lw=lw, pr=pr, mychunk=mychunk:
                                        e.matmul(
                                            lg[mychunk % 2][:, ls:ls + lw],
                                            Krs[pr][:, jj * 128:(jj + 1) * 128],
                                            QTs[:, h * 2048 + q0 + ls:
                                                h * 2048 + q0 + ls + lw],
                                            start=True, stop=True),
                                        waits=w,
                                        inc="tL" if ls + lw == cw else None)
                            if state["pend"] is not None:
                                emit_pv(state["pend"])
                                state["pend"] = None
                            if state.get("pend_epi") is not None:
                                emit_epi(state["pend_epi"])
                                state["pend_epi"] = None
                            w = [("tL", plan.n("tL"))]
                            if mychunk >= 4:
                                w.append(("tPV", mychunk - 3))
                            plan.op("scalar",
                                    lambda e, mychunk=mychunk, cw=cw:
                                    e.activation(
                                        pT[mychunk % 4][:, 0:cw],
                                        lg[mychunk % 2][:, 0:cw], AF.Exp,
                                        scale=SCALE),
                                    waits=w, inc="nE")
                            first_of_window = False
                            if q0 == jj * 128:
                                plan.op("vector",
                                        lambda e, mychunk=mychunk, rs=rs:
                                        e.tensor_mul(
                                            pT[mychunk % 4][:, 0:128],
                                            pT[mychunk % 4][:, 0:128],
                                            mask_sb[:, rs * 128:
                                                    (rs + 1) * 128]),
                                        waits=[("nE", mychunk + 1)],
                                        inc="nM")
                                masked_cnt[mychunk] = plan.n("nM")
                            state["pend"] = (h, half, rs, jj, mychunk,
                                             q0, cw, pr, first_of_half,
                                             last_jj)
                            first_of_half = False
                            state["cn"] += 1
                        kv_guard[it] = state["cn"]
                        if it + 3 < H * 2 * 4:
                            kv_dma(it + 3)
                    if state["pend"] is not None:
                        emit_pv(state["pend"])
                        state["pend"] = None
                    hidx = h * 2 + half
                    plan.op("vector", lambda e: e.tensor_copy(
                        s_row[:, :], sums[0:1, 0:1024]),
                        waits=[("tPV", plan.n("tPV"))], inc="vRec")
                    state["pend_epi"] = (h, hs, hidx, plan.n("vRec"))
                tL_h[h] = plan.n("tL")
            if state.get("pend_epi") is not None:
                emit_epi(state["pend_epi"])
                state["pend_epi"] = None
            flash_tL = plan.n("tL")
            flash_tPV = plan.n("tPV")
            flash_nE = plan.n("nE")
            plan.emit(nc)

        with ExitStack() as c2:
            psO = [c2.enter_context(nc.psum_tensor(f"psO{i}", [128, 512], F32))
                   for i in range(4)]
            psSel = c2.enter_context(nc.psum_tensor("psSel", [2, 2048], F32))
            plan = Plan(sems)
            gate = [("nEvac", 32), ("tL", flash_tL), ("tPV", flash_tPV),
                    ("nE", flash_nE)]

            for nn in range(4):
                w = [("sIn", plan.n("sIn")), ("gMisc", gm_all)]
                if nn == 0:
                    w += gate
                if nn >= 2:
                    w.append(("vOut", nn - 1))
                plan.op("tensor", lambda e, nn=nn: e.matmul(
                    psO[nn % 4][:, :], ones1[:, :],
                    out8[0][0:1, nn * 512:(nn + 1) * 512],
                    start=True, stop=True),
                    waits=w, inc="tO")
                plan.op("vector", lambda e, nn=nn: e.tensor_copy(
                    Bo[:, nn * 512:(nn + 1) * 512], psO[nn % 4][:, :]),
                    waits=[("tO", nn + 1)], inc="vOut")

            ptoks = []
            for c in range(8):
                w = [] if c < 2 else [("tSel", c - 1)]
                ptoks.append(plan.dma("sync", ptmp[c % 2][:, :],
                                      pall[c * 8:(c + 1) * 8, :],
                                      inc="sP2", waits=w))
            for c in range(8):
                for nn in range(4):
                    w = []
                    if nn == 0:
                        w = [ptoks[c]]
                        if c == 0:
                            w.append(("sIn", 48))
                    plan.op("tensor", lambda e, c=c, nn=nn: e.matmul(
                        psSel[:, nn * 512:(nn + 1) * 512], sel_sb[0:8, :],
                        ptmp[c % 2][0:8, nn * 512:(nn + 1) * 512],
                        start=(c == 0), stop=(c == 7)),
                        waits=w, inc="tSel" if nn == 3 else None)

            wos_sl = [wos, QTs[:, 16 * 512:32 * 512]]
            for f4 in range(4):
                if f4 == 0:
                    w = [("nEvac", 16)]
                elif f4 == 1:
                    w = [("nEvac", 24)]
                else:
                    w = [("tO", 4 + (f4 - 1) * 16)]
                src = wo.rearrange("(cc p) f -> p cc f", p=128)[
                    :, :, f4 * 512:(f4 + 1) * 512]
                cast = plan.dma("gpsimd", wos_sl[f4 % 2][:, :].rearrange(
                    "p (cc f) -> p cc f", f=512), src, inc="gWo", waits=w)
                for tt in range(NT):
                    gl = f4 * 16 + tt
                    gi = plan.n("tO")
                    par = gi % 4
                    for cc in range(16):
                        w = []
                        if cc == 0:
                            w = [cast, ("vOut", max(4, plan.n("vOut") - 3))]
                        plan.op("tensor",
                                lambda e, f4=f4, tt=tt, cc=cc, par=par:
                                e.matmul(
                                    psO[par][:, :],
                                    attnT[:, cc * 2048 + tt * 128:
                                          cc * 2048 + (tt + 1) * 128],
                                    wos_sl[f4 % 2][:, cc * 512:(cc + 1) * 512],
                                    start=(cc == 0), stop=(cc == 15)),
                                waits=w, inc="tO" if cc == 15 else None)
                    dxw = [("vOut", max(4, plan.n("vOut") - 1))] if gl >= 2 else []
                    dx = plan.dma("scalar", xt[gl % 2][:, :],
                                  x[tt * 128:(tt + 1) * 128,
                                    f4 * 512:(f4 + 1) * 512],
                                  inc="sX2", waits=dxw)
                    opar = gl % 2
                    w = [("tO", gi + 1), dx]
                    if gl >= 2:
                        w.append(("sO", plan.n("sO") - 16))
                    plan.op("vector", lambda e, par=par, opar=opar:
                            e.scalar_tensor_tensor(
                                outsb[opar][:, :], psO[par][:, :], 1.0,
                                xt[opar][:, :], op0=ALU.mult, op1=ALU.add),
                            waits=w)
                    plan.op("vector", lambda e, f4=f4, opar=opar:
                            e.tensor_add(outsb[opar][:, :], outsb[opar][:, :],
                                         Bo[:, f4 * 512:(f4 + 1) * 512]))
                    if tt == 0:
                        plan.op("vector", lambda e, f4=f4, opar=opar:
                                e.tensor_add(
                                    outsb[opar][0:2, :], outsb[opar][0:2, :],
                                    psSel[:, f4 * 512:(f4 + 1) * 512]),
                                waits=[("tSel", 8)])
                    plan.op("vector", lambda e: e.engine_nop(), inc="vOut")
                    plan.dma("sync", o[tt * 128:(tt + 1) * 128,
                                      f4 * 512:(f4 + 1) * 512],
                             outsb[opar][:, :], inc="sO",
                             waits=[("vOut", plan.n("vOut"))])
            plan.barrier("sync", [("sO", plan.n("sO"))])
            plan.emit(nc)
    return nc



_NC_CACHE = {}


def _get_nc(which):
    if which not in _NC_CACHE:
        _NC_CACHE[which] = build_l1() if which == "l1" else build_l2()
    return _NC_CACHE[which]


def _build_masks(rc):
    m = np.zeros((4, 128, 128), dtype=bf16)
    s = np.arange(128)[:, None]
    q = np.arange(128)[None, :]
    for rs in range(4):
        off = 1 if rc < rs else 0
        m[rs] = (q >= s + off).astype(bf16)
    return m


def kernel(**inputs):
    xfull = np.asarray(inputs["L0_attn_query"], dtype=np.float32)
    wq = np.asarray(inputs["L0_attn_w_q"], np.float32)
    bq = np.asarray(inputs["L0_attn_b_q"], np.float32)
    wk = np.asarray(inputs["L0_attn_w_k"], np.float32)
    bk = np.asarray(inputs["L0_attn_b_k"], np.float32)
    wv = np.asarray(inputs["L0_attn_w_v"], np.float32)
    bv = np.asarray(inputs["L0_attn_b_v"], np.float32)
    wo = np.asarray(inputs["L0_attn_w_o"], np.float32)
    bo = np.asarray(inputs["L0_attn_b_o"], np.float32)
    w1 = np.asarray(inputs["L0_ffn_w1"], np.float32)
    w3 = np.asarray(inputs["L0_ffn_w3"], np.float32)
    w2 = np.asarray(inputs["L0_ffn_w2"], np.float32)
    wr = np.asarray(inputs["L0_ffn_wr"], np.float32)

    nc1 = _get_nc("l1")
    in1 = []
    xs = []
    for c in range(8):
        b, r = c // 4, c % 4
        xc = np.ascontiguousarray(xfull[b, r::4, :])
        xs.append(xc)
        in1.append(dict(x=xc, wq=wq, bq=bq, wk=wk, bk=bk, wv=wv, bv=bv,
                        w1=w1, w3=w3, w2=w2, wr=wr))
    r1 = run_bass_kernel_spmd(nc1, in1, core_ids=list(range(8)))

    KTs = [np.asarray(r1.results[c]["KT"]) for c in range(8)]
    Vs = [np.asarray(r1.results[c]["V"]) for c in range(8)]
    QTs = [np.asarray(r1.results[c]["QT"]) for c in range(8)]
    ps = [np.asarray(r1.results[c]["p"], np.float32) for c in range(8)]

    pall = np.ascontiguousarray(np.stack(ps, axis=0).reshape(64, D))
    nc2 = _get_nc("l2")
    in2 = []
    for c in range(8):
        b, r = c // 4, c % 4
        KTgc = np.ascontiguousarray(
            np.concatenate(KTs[b * 4:(b + 1) * 4], axis=0))
        Vgc = np.ascontiguousarray(
            np.concatenate(Vs[b * 4:(b + 1) * 4], axis=0))
        selm = np.zeros((8, 2), np.float32)
        if b == 0:
            selm[r, 0] = 1.0
            selm[r + 4, 1] = 1.0
        in2.append(dict(KTg=KTgc, Vg=Vgc, QT=QTs[c], x=xs[c], wo=wo, bo=bo,
                        masks=_build_masks(r), sel=selm, pall=pall))
    r2 = run_bass_kernel_spmd(nc2, in2, core_ids=list(range(8)))

    out = np.empty((2, 8192, D), np.float32)
    for c in range(8):
        b, r = c // 4, c % 4
        out[b, r::4, :] = np.asarray(r2.results[c]["o"], np.float32)
    return out


# version 24
# speedup vs baseline: 1.0173x; 1.0173x over previous
import numpy as np
import ml_dtypes
from contextlib import ExitStack

import concourse.bass as bass
import concourse.mybir as mybir
from concourse.bass_utils import run_bass_kernel_spmd

F32 = mybir.dt.float32
BF16 = mybir.dt.bfloat16
AF = mybir.ActivationFunctionType
ALU = mybir.AluOpType
bf16 = ml_dtypes.bfloat16

D = 2048
H = 16
HS = 128
NT = 16
DC = 16
EPS = 1e-5
SCALE = float(1.0 / np.sqrt(np.float32(HS)))

ENGINES = ("sync", "gpsimd", "tensor", "vector", "scalar")
DEBUG = False


class Plan:

    def __init__(self, sems):
        self.ops = {e: [] for e in ENGINES}
        self.sems = sems
        self.last_tok = sems.setdefault("__last_tok", {})

    def n(self, name):
        return self.sems[name][1]

    def op(self, eng, fn, waits=(), inc=None, inc_by=1, is_dma=False):
        if inc is not None:
            self.sems[inc][1] += inc_by
        self.ops[eng].append((fn, tuple(waits), inc, inc_by, is_dma))
        return (inc, self.sems[inc][1]) if inc is not None else None

    def dma(self, eng, out, in_, inc, waits=(), **kw):
        def fn(e, o=out, i=in_, kw=kw):
            return e.dma_start(out=o, in_=i, **kw)
        return self.op(eng, fn, waits, inc, 16, is_dma=True)

    def barrier(self, eng, waits):
        self.ops[eng].append((None, tuple(waits), None, 0, False))

    def emit(self, nc):
        with nc.Block() as block:
            for eng in ENGINES:
                ops = self.ops[eng]
                if not ops:
                    continue
                selfs = ("__self_" + eng) if eng in ("vector", "scalar") else None

                def body(e, _ops=ops, _selfs=selfs, _eng=eng):
                    sh = self.sems[_selfs] if _selfs else None
                    last = self.last_tok.get(_eng)
                    for fn, waits, inc, inc_by, is_dma in _ops:
                        for (sname, v) in waits:
                            e.wait_ge(self.sems[sname][0], v)
                        if fn is None:
                            continue
                        if sh is not None and last is not None and not is_dma:
                            e.wait_ge(last[0], last[1])
                        ins = fn(e)
                        if inc is not None:
                            s = self.sems[inc]
                            s[2] += inc_by
                            ins.then_inc(s[0], inc_by)
                            if sh is not None and not is_dma:
                                last = (s[0], s[2])
                        elif sh is not None and not is_dma:
                            sh[2] += 1
                            ins.then_inc(sh[0], 1)
                            last = (sh[0], sh[2])
                    if _selfs:
                        self.last_tok[_eng] = last
                getattr(block, eng)(body)


def _mk_sems(ctx, nc, names):
    names = list(names) + ["__self_vector", "__self_scalar"]
    return {n: [ctx.enter_context(nc.semaphore(n)), 0, 0] for n in names}


def _ident(plan, ident, sem):
    plan.op("gpsimd", lambda e: e.memset(ident[:, :], 0.0), inc=sem)
    plan.op("gpsimd", lambda e: e.affine_select(
        out=ident[:, :], in_=ident[:, :], compare_op=ALU.not_equal,
        fill=1.0, base=0, pattern=[[-1, 128]], channel_multiplier=1),
        inc=sem)



def build_l1():
    nc = bass.Bass("TRN2", target_bir_lowering=False, debug=False)
    x = nc.dram_tensor("x", [2048, D], F32, kind="ExternalInput").ap()
    wq = nc.dram_tensor("wq", [D, D], F32, kind="ExternalInput").ap()
    bq = nc.dram_tensor("bq", [D], F32, kind="ExternalInput").ap()
    wk = nc.dram_tensor("wk", [D, D], F32, kind="ExternalInput").ap()
    bk = nc.dram_tensor("bk", [D], F32, kind="ExternalInput").ap()
    wv = nc.dram_tensor("wv", [D, D], F32, kind="ExternalInput").ap()
    bv = nc.dram_tensor("bv", [D], F32, kind="ExternalInput").ap()
    w1 = nc.dram_tensor("w1", [D, D], F32, kind="ExternalInput").ap()
    w3 = nc.dram_tensor("w3", [D, D], F32, kind="ExternalInput").ap()
    w2 = nc.dram_tensor("w2", [D, D], F32, kind="ExternalInput").ap()
    wr = nc.dram_tensor("wr", [8, D], F32, kind="ExternalInput").ap()
    KT_o = nc.dram_tensor("KT", [D, 2048], BF16, kind="ExternalOutput").ap()
    V_o = nc.dram_tensor("V", [2048, D], BF16, kind="ExternalOutput").ap()
    QT_o = nc.dram_tensor("QT", [D, 2048], BF16, kind="ExternalOutput").ap()
    p_o = nc.dram_tensor("p", [8, D], F32, kind="ExternalOutput").ap()
    if DEBUG:
        dbg_xnT = nc.dram_tensor("dbg_xnT", [128, DC * 2048], BF16,
                                 kind="ExternalOutput").ap()
        dbg_W = nc.dram_tensor("dbg_W", [128, 128], BF16,
                               kind="ExternalOutput").ap()
        dbg_h1 = nc.dram_tensor("dbg_h1", [128, 2048], BF16,
                                kind="ExternalOutput").ap()
        dbg_h2 = nc.dram_tensor("dbg_h2", [128, 2048], BF16,
                                kind="ExternalOutput").ap()

    with ExitStack() as ctx:
        E = ctx.enter_context
        xnT = E(nc.sbuf_tensor("xnT", [128, DC * 2048], BF16))
        h1T = E(nc.sbuf_tensor("h1T", [128, 16 * 2048], BF16))
        wslab = [E(nc.sbuf_tensor(f"wslab{i}", [128, 2048], BF16)) for i in range(3)]
        wtsl = [E(nc.sbuf_tensor(f"wtsl{i}", [128, 2048], BF16)) for i in range(2)]
        stage = [E(nc.sbuf_tensor(f"stage{i}", [128, 2048], BF16)) for i in range(2)]
        h3t = [E(nc.sbuf_tensor(f"h3t{i}", [128, 512], BF16)) for i in range(2)]
        xt = [E(nc.sbuf_tensor("xt0", [128, 2048], F32)), None]
        xnb = [E(nc.sbuf_tensor(f"xnb{i}", [128, 2048], BF16)) for i in range(2)]
        xcen = E(nc.sbuf_tensor("xcen", [128, 2048], F32))
        ident = E(nc.sbuf_tensor("ident", [128, 128], BF16))
        eps_t = E(nc.sbuf_tensor("eps_t", [128, 1], F32))
        bqb = E(nc.sbuf_tensor("bqb", [128, 16], F32))
        bkb = E(nc.sbuf_tensor("bkb", [128, 16], F32))
        Bv = E(nc.sbuf_tensor("Bv", [128, 2048], F32))
        xt[1] = Bv
        ones1 = E(nc.sbuf_tensor("ones1", [1, 128], F32))
        wr_bf = E(nc.sbuf_tensor("wr_bf", [8, 2048], BF16))
        wrT = E(nc.sbuf_tensor("wrT", [128, 16 * 8], BF16))
        Wsb = E(nc.sbuf_tensor("Wsb", [128, 16 * 8], BF16))
        out8 = E(nc.sbuf_tensor("out8", [8, 2048], F32))
        bvrow = out8[0:1, :]
        s1 = [E(nc.sbuf_tensor(f"s1_{i}", [128, 1], F32)) for i in range(2)]
        mmean = [E(nc.sbuf_tensor(f"m_{i}", [128, 1], F32)) for i in range(2)]
        m2t = [E(nc.sbuf_tensor(f"m2_{i}", [128, 1], F32)) for i in range(2)]
        vvar = [E(nc.sbuf_tensor(f"vv_{i}", [128, 1], F32)) for i in range(2)]
        sacc = [E(nc.sbuf_tensor(f"sacc_{i}", [128, 1], F32)) for i in range(2)]
        rstd = [E(nc.sbuf_tensor(f"rstd_{i}", [128, 1], F32)) for i in range(2)]
        r_eq1 = E(nc.sbuf_tensor("r_eq1", [128, 8], F32))
        r_eq2 = E(nc.sbuf_tensor("r_eq2", [128, 8], F32))
        r_l2 = E(nc.sbuf_tensor("r_l2", [128, 8], F32))
        r_t1 = E(nc.sbuf_tensor("r_t1", [128, 8], F32))
        r_m1 = E(nc.sbuf_tensor("r_m1", [128, 1], F32))
        r_m2 = E(nc.sbuf_tensor("r_m2", [128, 1], F32))
        r_d = E(nc.sbuf_tensor("r_d", [128, 1], F32))
        r_g1 = E(nc.sbuf_tensor("r_g1", [128, 1], F32))
        r_g2 = E(nc.sbuf_tensor("r_g2", [128, 1], F32))

        sems = _mk_sems(ctx, nc, [
            "sX", "eSq", "vStat", "eRstd", "vXN", "vMn", "eXc", "tTr",
            "vTr", "gMisc",
            "sB", "gK", "tK", "eK", "sKout", "gV", "tV", "vV", "sV",
            "tBv", "vBv", "gWr", "tWr", "vWrT", "tR", "vRm", "eRg", "vRt",
            "gF", "tFtr", "vFtr", "tH", "eH", "vH3", "tS", "vO8", "sP",
        ])

        with ExitStack() as c2:
            psA = [c2.enter_context(nc.psum_tensor(f"psA{i}", [128, 128], BF16))
                   for i in range(2)]
            psB = [c2.enter_context(nc.psum_tensor(f"psB{i}", [128, 512], F32))
                   for i in range(4)]
            psR = c2.enter_context(nc.psum_tensor("psR", [128, 8], F32))
            psTB = psA
            plan = Plan(sems)
            _ident(plan, ident, "gMisc")
            plan.op("gpsimd", lambda e: e.memset(ones1[:, :], 1.0), inc="gMisc")
            plan.op("gpsimd", lambda e: e.memset(eps_t[:, :], EPS), inc="gMisc")
            for (bsb, bsrc) in ((bkb, bk), (bqb, bq)):
                for c in range(16):
                    plan.dma("scalar", bsb[:, c:c + 1],
                             bsrc[c * 128:(c + 1) * 128].rearrange(
                                 "(p o) -> p o", o=1), inc="sB")
            plan.dma("scalar", bvrow,
                     bv.rearrange("(o f) -> o f", o=1), inc="sB")

            gtr = 0
            for tt in range(NT):
                p = tt % 2
                waits = [("eSq", tt - 1),
                         ("vXN", tt - 1)] if tt >= 2 else []
                dx = plan.dma("sync" if tt % 2 == 0 else "scalar",
                              xt[p][:, :],
                              x[tt * 128:(tt + 1) * 128, :], inc="sX",
                              waits=waits)
                plan.op("vector", lambda e, p=p: e.reduce_sum(
                    s1[p][:, :], xt[p][:, :], axis=mybir.AxisListType.X),
                    waits=[dx])
                plan.op("vector", lambda e, p=p: e.tensor_scalar_mul(
                    mmean[p][:, :], s1[p][:, :], -1.0 / 2048.0), inc="vMn")
                sq_w = [dx] + ([("vStat", tt - 1)] if tt >= 2 else [])
                sq = plan.op("scalar", lambda e, p=p: e.activation(
                    stage[0][:, :], xt[p][:, :], AF.Square,
                    accum_out=sacc[p][:, :]), waits=sq_w, inc="eSq")
                plan.op("vector", lambda e, p=p: e.tensor_scalar_mul(
                    s1[p][:, :], sacc[p][:, :], 1.0 / 2048.0), waits=[sq])
                plan.op("vector", lambda e, p=p: e.tensor_mul(
                    m2t[p][:, :], mmean[p][:, :], mmean[p][:, :]))
                st = plan.op("vector", lambda e, p=p: e.tensor_sub(
                    vvar[p][:, :], s1[p][:, :], m2t[p][:, :]), inc="vStat")
                ln_w = [st] + ([("gMisc", 4)] if tt == 0 else [])
                plan.op("scalar", lambda e, p=p: e.activation(
                    m2t[p][:, :], vvar[p][:, :], AF.Ln, bias=eps_t[:, :]),
                    waits=ln_w)
                plan.op("scalar", lambda e, p=p: e.activation(
                    rstd[p][:, :], m2t[p][:, :], AF.Exp, scale=-0.5),
                    inc="eRstd")
                xc_w = [("vMn", tt + 1)]
                if tt >= 1:
                    xc_w.append(("vXN", tt))
                xc = plan.op("scalar", lambda e, p=p: e.activation(
                    xcen[:, :], xt[p][:, :], AF.Identity,
                    bias=mmean[p][:, :]), waits=xc_w, inc="eXc")
                xn_w = [xc, ("eRstd", tt + 1)] +                     ([("tTr", (tt - 2) * 16 + 16)] if tt >= 2 else [])
                plan.op("vector", lambda e, p=p: e.tensor_scalar_mul(
                    xnb[p][:, :], xcen[:, :], rstd[p][:, :]),
                    waits=xn_w, inc="vXN")
                for dc in range(DC):
                    g = gtr
                    w = [("vXN", tt + 1), ("gMisc", 3)]
                    if g >= 2:
                        w.append(("vTr", g - 1))
                    plan.op("tensor", lambda e, p=p, dc=dc, g=g: e.transpose(
                        psA[g % 2][:, :], xnb[p][:, dc * 128:(dc + 1) * 128],
                        ident[:, :]), waits=w, inc="tTr")
                    plan.op("vector", lambda e, tt=tt, dc=dc, g=g: e.tensor_copy(
                        xnT[:, dc * 2048 + tt * 128: dc * 2048 + (tt + 1) * 128],
                        psA[g % 2][:, :]), waits=[("tTr", g + 1)], inc="vTr")
                    gtr += 1


            def kq_phase(wsrc, bsb, out_ap, mm_gate, cast_guard,
                         rdy=lambda fc, t4: []):
                base_tK = plan.n("tK")
                base_sK = plan.n("sKout")
                for fc in range(16):
                    if fc >= 2:
                        cw = [("tK", base_tK + 4 * (fc - 1))]
                    else:
                        cw = list(cast_guard)
                    src = wsrc.rearrange("(dc p) f -> p dc f", p=128)[
                        :, :, fc * 128:(fc + 1) * 128]
                    dst = wslab[fc % 2][:, :].rearrange(
                        "p (dc f) -> p dc f", f=128)
                    cast = plan.dma("gpsimd", dst, src, inc="gK", waits=cw)
                    for tp in range(2):
                        t4a, t4b = 2 * tp, 2 * tp + 1
                        gi = plan.n("tK")
                        pb = ((gi // 2) % 2) * 2
                        for dc in range(DC):
                            for sl, t4 in ((0, t4a), (1, t4b)):
                                w = []
                                if dc == 0 and sl == 0:
                                    w = [cast, ("eK", max(0, gi - 2))]
                                    if fc == 0 and tp == 0:
                                        w += mm_gate
                                    w += rdy(fc, t4b)
                                plan.op("tensor",
                                        lambda e, fc=fc, t4=t4, dc=dc,
                                        bk=pb + sl: e.matmul(
                                            psB[bk][:, :],
                                            wslab[fc % 2][:, dc * 128:
                                                          (dc + 1) * 128],
                                            xnT[:, dc * 2048 + t4 * 512:
                                                dc * 2048 + (t4 + 1) * 512],
                                            start=(dc == 0),
                                            stop=(dc == DC - 1)),
                                        waits=w,
                                        inc="tK" if dc == DC - 1 else None)
                        for off, t4 in ((1, t4a), (2, t4b)):
                            w = [("tK", gi + off)]
                            if fc == 0 and tp == 0 and off == 1 \
                                    and base_sK == 0:
                                w.append(("sB", 528))
                            if tp == 0 and off == 1:
                                if fc >= 2:
                                    w.append(("sKout", base_sK + 16 * (fc - 1)))
                                elif base_sK > 0:
                                    w.append(("sKout", base_sK))
                            plan.op("scalar",
                                    lambda e, fc=fc, t4=t4, bk=pb + off - 1,
                                    bsb=bsb: e.activation(
                                        stage[fc % 2][:, t4 * 512:
                                                      (t4 + 1) * 512],
                                        psB[bk][:, :], AF.Identity,
                                        bias=bsb[:, fc:fc + 1]),
                                    waits=w, inc="eK")
                    plan.dma("sync", out_ap[fc * 128:(fc + 1) * 128, :],
                             stage[fc % 2][:, :], inc="sKout",
                             waits=[("eK", plan.n("eK"))])

            if DEBUG:
                plan.dma("sync", dbg_xnT[:, :], xnT[:, :], inc="sP",
                         waits=[("vTr", 256)])
            kq_phase(wk, bkb, KT_o, [], [],
                     rdy=lambda fc, t4: (
                         [("vTr", (4 * t4 + 4) * 16)] if fc == 0 else []))

            for nn in range(4):
                w = [("gMisc", 3), ("sB", plan.n("sB")),
                     ("eK", plan.n("eK"))]
                if nn >= 2:
                    w.append(("vBv", nn - 1))
                plan.op("tensor", lambda e, nn=nn: e.matmul(
                    psB[nn % 2][:, :], ones1[:, :],
                    bvrow[:, nn * 512:(nn + 1) * 512], start=True, stop=True),
                    waits=w, inc="tBv")
                plan.op("vector", lambda e, nn=nn: e.tensor_copy(
                    Bv[:, nn * 512:(nn + 1) * 512], psB[nn % 2][:, :]),
                    waits=[("tBv", nn + 1)], inc="vBv")

            wvslab = [h1T[:, 0:16 * 512], h1T[:, 16 * 512:32 * 512]]
            for f4 in range(4):
                if f4 >= 2:
                    w = [("tV", 16 * (f4 - 1))]
                else:
                    w = [("vTr", 256)]
                src = wv.rearrange("(dc p) f -> p dc f", p=128)[
                    :, :, f4 * 512:(f4 + 1) * 512]
                dst = wvslab[f4 % 2].rearrange("p (dc f) -> p dc f", f=512)
                cast = plan.dma("gpsimd", dst, src, inc="gV", waits=w)
                for tt in range(NT):
                    gi = plan.n("tV")
                    par = gi % 2
                    for dc in range(DC):
                        w = []
                        if dc == 0:
                            w = [cast, ("vV", max(0, gi - 1))]
                            if gi == 0:
                                w.append(("eK", 64))
                        plan.op("tensor",
                                lambda e, f4=f4, tt=tt, dc=dc, par=par:
                                e.matmul(
                                    psB[par][:, :],
                                    xnT[:, dc * 2048 + tt * 128:
                                        dc * 2048 + (tt + 1) * 128],
                                    wvslab[f4 % 2][:, dc * 512:(dc + 1) * 512],
                                    start=(dc == 0), stop=(dc == DC - 1)),
                                waits=w, inc="tV" if dc == DC - 1 else None)
                    w = [("tV", gi + 1), ("vBv", 4)]
                    if gi >= 2:
                        w.append(("sV", 16 * (gi - 1)))
                    plan.op("vector", lambda e, f4=f4, tt=tt, par=par:
                            e.scalar_tensor_tensor(
                                h3t[gi % 2][:, :],
                                psB[par][:, :], 1.0,
                                Bv[:, f4 * 512:(f4 + 1) * 512],
                                op0=ALU.mult, op1=ALU.add),
                            waits=w, inc="vV")
                    plan.dma("sync",
                             V_o[tt * 128:(tt + 1) * 128,
                                 f4 * 512:(f4 + 1) * 512],
                             h3t[gi % 2][:, :], inc="sV",
                             waits=[("vV", plan.n("vV"))])

            plan.dma("gpsimd", wr_bf[:, :], wr[:, :], inc="gWr",
                     waits=[("vXN", 16)])
            for dc in range(DC):
                w = [("gWr", 16), ("gMisc", 3)]
                if dc == 0:
                    w.append(("vTr", 256))
                if dc >= 2:
                    w.append(("vWrT", dc - 1))
                plan.op("tensor", lambda e, dc=dc: e.transpose(
                    psTB[dc % 2][:, 0:8], wr_bf[:, dc * 128:(dc + 1) * 128],
                    ident[0:8, 0:8]), waits=w, inc="tWr")
                plan.op("vector", lambda e, dc=dc: e.tensor_copy(
                    wrT[:, dc * 8:(dc + 1) * 8], psTB[dc % 2][:, 0:8]),
                    waits=[("tWr", dc + 1)], inc="vWrT")
            for tt in range(NT):
                for dc in range(DC):
                    w = []
                    if dc == 0:
                        w = [("vWrT", 16)]
                        if tt:
                            w.append(("vRt", tt))
                    plan.op("tensor", lambda e, tt=tt, dc=dc: e.matmul(
                        psR[:, :],
                        xnT[:, dc * 2048 + tt * 128: dc * 2048 + (tt + 1) * 128],
                        wrT[:, dc * 8:(dc + 1) * 8],
                        start=(dc == 0), stop=(dc == DC - 1)),
                        waits=w, inc="tR" if dc == DC - 1 else None)
                plan.op("vector", lambda e, tt=tt: e.reduce_max(
                    r_m1[:, :], psR[:, :], axis=mybir.AxisListType.X),
                    waits=[("tR", tt + 1)])
                plan.op("vector", lambda e: e.tensor_single_scalar(
                    r_eq1[:, :], psR[:, :], r_m1[:, :], ALU.is_equal))
                plan.op("vector", lambda e: e.scalar_tensor_tensor(
                    r_l2[:, :], r_eq1[:, :], -1e30, psR[:, :],
                    op0=ALU.mult, op1=ALU.add))
                plan.op("vector", lambda e: e.reduce_max(
                    r_m2[:, :], r_l2[:, :], axis=mybir.AxisListType.X))
                plan.op("vector", lambda e: e.tensor_single_scalar(
                    r_eq2[:, :], r_l2[:, :], r_m2[:, :], ALU.is_equal))
                plan.op("vector", lambda e: e.tensor_sub(
                    r_d[:, :], r_m1[:, :], r_m2[:, :]), inc="vRm")
                plan.op("scalar", lambda e, tt=tt: e.activation(
                    r_g1[:, :], r_d[:, :], AF.Sigmoid),
                    waits=[("vRm", tt + 1)])
                plan.op("scalar", lambda e: e.activation(
                    r_g2[:, :], r_d[:, :], AF.Sigmoid, scale=-1.0), inc="eRg")
                plan.op("vector", lambda e: e.tensor_scalar_mul(
                    r_t1[:, :], r_eq1[:, :], r_g1[:, :]),
                    waits=[("eRg", tt + 1)])
                plan.op("vector", lambda e, tt=tt: e.scalar_tensor_tensor(
                    Wsb[:, tt * 8:(tt + 1) * 8], r_eq2[:, :], r_g2[:, :],
                    r_t1[:, :], op0=ALU.mult, op1=ALU.add), inc="vRt")

            kq_phase(wq, bqb, QT_o, [("vV", 64)], [("tK", 64)])

            plan.emit(nc)

        with ExitStack() as c2:
            psC = [c2.enter_context(nc.psum_tensor(f"psC{i}", [128, 512], F32))
                   for i in range(4)]
            psT = [c2.enter_context(nc.psum_tensor(f"psT{i}", [128, 128], BF16))
                   for i in range(4)]
            plan = Plan(sems)

            def hstage(wsrc, in_sb, out_sb, act, mm_gate, cast_guard,
                       tr_gate=()):
                base_tFtr = plan.n("tFtr")
                base_tH = plan.n("tH")
                for ic in range(16):
                    if ic >= 3:
                        cw = [("tFtr", base_tFtr + 16 * (ic - 2))]
                    else:
                        cw = list(cast_guard[min(ic, len(cast_guard) - 1)])
                    cast = plan.dma("gpsimd", wslab[ic % 3][:, :],
                                    wsrc[ic * 128:(ic + 1) * 128, :],
                                    inc="gF", waits=cw)
                    for dc in range(DC):
                        g = plan.n("tFtr")
                        w = [cast]
                        if ic == 0 and dc == 0:
                            w += list(tr_gate)
                        if g >= 4:
                            w.append(("vFtr", g - 3))
                        plan.op("tensor", lambda e, ic=ic, dc=dc, g=g:
                                e.transpose(
                                    psT[g % 4][:, :],
                                    wslab[ic % 3][:, dc * 128:(dc + 1) * 128],
                                    ident[:, :]), waits=w, inc="tFtr")
                        evw = [("tFtr", g + 1)]
                        if ic == 0 and dc == 0:
                            evw += mm_gate
                        if dc == 0:
                            if ic >= 2:
                                evw.append(("tH", base_tH + 4 * (ic - 1)))
                            elif base_tH > 0:
                                evw.append(("tH", base_tH))
                        plan.op("vector", lambda e, ic=ic, dc=dc, g=g:
                                e.tensor_copy(
                                    wtsl[ic % 2][:, dc * 128:(dc + 1) * 128],
                                    psT[g % 4][:, :]), waits=evw, inc="vFtr")
                    for tp in range(2):
                        t4a, t4b = 2 * tp, 2 * tp + 1
                        gi = plan.n("tH")
                        pb = ((gi // 2) % 2) * 2
                        for dc in range(DC):
                            for sl, t4 in ((0, t4a), (1, t4b)):
                                w = []
                                if dc == 0 and sl == 0:
                                    w = [("vFtr", plan.n("vFtr")),
                                         ("eH", max(0, gi - 2))]
                                plan.op("tensor",
                                        lambda e, ic=ic, t4=t4, dc=dc,
                                        bk=pb + sl: e.matmul(
                                            psC[bk][:, :],
                                            wtsl[ic % 2][:, dc * 128:
                                                         (dc + 1) * 128],
                                            in_sb[:, dc * 2048 + t4 * 512:
                                                  dc * 2048 + (t4 + 1) * 512],
                                            start=(dc == 0),
                                            stop=(dc == DC - 1)),
                                        waits=w,
                                        inc="tH" if dc == DC - 1 else None)
                        for off, t4 in ((1, t4a), (2, t4b)):
                            plan.op("scalar",
                                    lambda e, ic=ic, t4=t4, bk=pb + off - 1:
                                    e.activation(
                                        out_sb[:, ic * 2048 + t4 * 512:
                                               ic * 2048 + (t4 + 1) * 512],
                                        psC[bk][:, :], act),
                                    waits=[("tH", gi + off)], inc="eH")

            if DEBUG:
                plan.dma("sync", dbg_W[:, :], Wsb[:, :], inc="sP",
                         waits=[("vRt", 16)])
            gateC = [("eK", plan.n("eK")), ("vV", plan.n("vV")),
                     ("vRt", 16), ("sV", plan.n("sV"))]
            hstage(w1, xnT, h1T, AF.Silu, gateC,
                   [[("tK", 124)], [("tK", 128)], []],
                   tr_gate=[("eK", plan.n("eK")), ("vRt", 16)])
            base_h2_eH = plan.n("eH")
            hstage(w3, h1T, xnT, AF.Silu,
                   [("tH", plan.n("tH")), ("tR", 16)],
                   [[("tFtr", 256)]])

            if DEBUG:
                plan.dma("sync", dbg_h1[:, :], h1T[:, 0:2048], inc="sP",
                         waits=[("eH", 64)])
                plan.dma("sync", dbg_h2[:, :], xnT[:, 0:2048], inc="sP",
                         waits=[("eH", 128)])
            plan.emit(nc)

        with ExitStack() as c3:
            psC = [c3.enter_context(nc.psum_tensor(f"psC3{i}", [128, 512],
                                                   F32)) for i in range(2)]
            psT = [c3.enter_context(nc.psum_tensor(f"psT3{i}", [128, 128],
                                                   BF16)) for i in range(2)]
            ps8 = c3.enter_context(nc.psum_tensor("ps8", [8, 2048], F32))
            plan = Plan(sems)
            h2T = xnT
            w2Tsl = [h1T[:, 0:16 * 512], h1T[:, 16 * 512:32 * 512]]
            base3_tH = plan.n("tH")
            base3_tFtr = plan.n("tFtr")
            eH_all = plan.n("eH")
            first_scatter = True
            for f4 in range(4):
                for k in range(4):
                    si3 = f4 * 4 + k
                    if si3 >= 2:
                        cw = [("tFtr", base3_tFtr + 16 * (si3 - 1))]
                    else:
                        cw = [("tFtr", base3_tFtr)]
                    cast = plan.dma("gpsimd", wslab[k % 2][:, :],
                                    w2[f4 * 512 + k * 128:
                                       f4 * 512 + (k + 1) * 128, :],
                                    inc="gF", waits=cw)
                    for ic in range(16):
                        g = plan.n("tFtr")
                        w = [cast]
                        if g == base3_tFtr:
                            w.append(("eH", eH_all))
                        if g >= 2:
                            w.append(("vFtr", g - 1))
                        plan.op("tensor", lambda e, k=k, ic=ic, g=g:
                                e.transpose(
                                    psT[g % 2][:, :],
                                    wslab[k % 2][:, ic * 128:(ic + 1) * 128],
                                    ident[:, :]), waits=w, inc="tFtr")
                        evw = [("tFtr", g + 1)]
                        if si3 == 0 and ic == 0:
                            evw.append(("tH", base3_tH))
                        if si3 >= 8 and ic == 0:
                            evw.append(("tH", base3_tH + (f4 - 1) * 16))
                        plan.op("vector", lambda e, f4=f4, k=k, ic=ic, g=g:
                                e.tensor_copy(
                                    w2Tsl[f4 % 2][:, ic * 512 + k * 128:
                                                  ic * 512 + (k + 1) * 128],
                                    psT[g % 2][:, :]), waits=evw, inc="vFtr")
                for tt in range(NT):
                    gl = f4 * 16 + tt
                    par = gl % 2
                    for ic in range(16):
                        w = []
                        if ic == 0:
                            w = [("vFtr", plan.n("vFtr"))]
                            if gl < 2:
                                w.append(("eH", eH_all))
                            else:
                                w.append(("vH3", gl - 1))
                        plan.op("tensor",
                                lambda e, f4=f4, tt=tt, ic=ic, par=par:
                                e.matmul(
                                    psC[par][:, :],
                                    h2T[:, ic * 2048 + tt * 128:
                                        ic * 2048 + (tt + 1) * 128],
                                    w2Tsl[f4 % 2][:, ic * 512:(ic + 1) * 512],
                                    start=(ic == 0), stop=(ic == 15)),
                                waits=w, inc="tH" if ic == 15 else None)
                    if tt > 0:
                        w = [("vH3", plan.n("vH3"))]
                        if first_scatter:
                            w.append(("vRt", 16))
                            first_scatter = False
                        plan.op("tensor", lambda e, f4=f4, tt=tt, gl=gl:
                                e.matmul(
                                    ps8[0:8, f4 * 512:(f4 + 1) * 512],
                                    Wsb[:, (tt - 1) * 8: tt * 8],
                                    h3t[(gl - 1) % 2][:, :],
                                    start=(tt - 1 == 0), stop=False),
                                waits=w, inc="tS")
                    w = [("tH", base3_tH + gl + 1)]
                    if gl >= 2:
                        w.append(("tS", plan.n("tS")))
                    plan.op("vector", lambda e, par=par: e.tensor_copy(
                        h3t[par][:, :], psC[par][:, :]), waits=w, inc="vH3")
                plan.op("tensor", lambda e, f4=f4: e.matmul(
                    ps8[0:8, f4 * 512:(f4 + 1) * 512],
                    Wsb[:, 15 * 8:16 * 8],
                    h3t[(f4 * 16 + 15) % 2][:, :],
                    start=False, stop=True),
                    waits=[("vH3", plan.n("vH3"))], inc="tS")
            plan.op("vector", lambda e: e.tensor_copy(out8[:, :], ps8[0:8, :]),
                    waits=[("tS", plan.n("tS"))], inc="vO8")
            plan.dma("sync", p_o[:, :], out8[:, :], inc="sP",
                     waits=[("vO8", 1)])
            plan.barrier("sync", [("sKout", plan.n("sKout")),
                                  ("sV", plan.n("sV")), ("sP", plan.n("sP"))])
            plan.emit(nc)
    return nc



def build_l2():
    nc = bass.Bass("TRN2", target_bir_lowering=False, debug=False)
    KTg = nc.dram_tensor("KTg", [4 * D, 2048], BF16, kind="ExternalInput").ap()
    Vg = nc.dram_tensor("Vg", [4 * 2048, D], BF16, kind="ExternalInput").ap()
    QT = nc.dram_tensor("QT", [D, 2048], BF16, kind="ExternalInput").ap()
    x = nc.dram_tensor("x", [2048, D], F32, kind="ExternalInput").ap()
    wo = nc.dram_tensor("wo", [D, D], F32, kind="ExternalInput").ap()
    bo = nc.dram_tensor("bo", [D], F32, kind="ExternalInput").ap()
    masks = nc.dram_tensor("masks", [4, 128, 128], BF16,
                           kind="ExternalInput").ap()
    sel = nc.dram_tensor("sel", [8, 2], F32, kind="ExternalInput").ap()
    pall = nc.dram_tensor("pall", [64, D], F32, kind="ExternalInput").ap()
    o = nc.dram_tensor("o", [2048, D], F32, kind="ExternalOutput").ap()

    with ExitStack() as ctx:
        E = ctx.enter_context
        QTs = E(nc.sbuf_tensor("QTs", [128, H * 2048], BF16))
        attnT = E(nc.sbuf_tensor("attnT", [128, H * 2048], BF16))
        Krs = [E(nc.sbuf_tensor(f"Krs{i}", [128, 2048], BF16)) for i in range(3)]
        Vrs = [E(nc.sbuf_tensor(f"Vrs{i}", [128, 2048], BF16)) for i in range(3)]
        pT = [E(nc.sbuf_tensor(f"pT{i}", [128, 1024], BF16)) for i in range(4)]
        mask_sb = E(nc.sbuf_tensor("mask_sb", [128, 4 * 128], BF16))
        xt = [E(nc.sbuf_tensor(f"xtL2{i}", [128, 512], F32)) for i in range(2)]
        Bo = E(nc.sbuf_tensor("Bo", [128, 2048], F32))
        outsb = [E(nc.sbuf_tensor(f"outsb{i}", [128, 512], F32))
                 for i in range(2)]
        ptmp = [E(nc.sbuf_tensor(f"ptmp{i}", [8, 2048], F32)) for i in range(2)]
        out8 = [E(nc.sbuf_tensor("out80", [8, 2048], F32))]
        s_row = None
        bc_sb = E(nc.sbuf_tensor("bc_sb", [128, 1024], F32))
        s_row = Bo[0:1, 1024:2048]
        lnv_sb = None
        ones1 = E(nc.sbuf_tensor("ones1L2", [1, 128], F32))
        ones_s = E(nc.sbuf_tensor("ones_s", [128, 1], BF16))
        sel_sb = E(nc.sbuf_tensor("sel_sb", [8, 2], F32))
        wos = QTs[:, 0:16 * 512]
        borow = out8[0][0:1, :]

        sems = _mk_sems(ctx, nc, [
            "gMisc", "sIn", "sKV", "tL", "nE", "nM", "tPV", "nEvac",
            "vRec", "tBc", "vBc", "gWo", "tO", "vOut", "sO", "tSel",
            "sX2", "sP2",
        ])

        with ExitStack() as c2:
            accT = c2.enter_context(nc.psum_tensor("accT", [128, 1024], F32))
            sums = c2.enter_context(nc.psum_tensor("sums", [128, 1024], F32))
            lg = [c2.enter_context(nc.psum_tensor(f"lg{i}", [128, 1024], F32))
                  for i in range(2)]
            plan = Plan(sems)
            plan.op("gpsimd", lambda e: e.memset(ones1[:, :], 1.0), inc="gMisc")
            plan.op("gpsimd", lambda e: e.memset(ones_s[:, :], 1.0), inc="gMisc")
            gm_all = plan.n("gMisc")
            plan.dma("sync", mask_sb[:, :].rearrange("p (r q) -> p r q", q=128),
                     masks.rearrange("r p q -> p r q"), inc="sIn")
            plan.dma("sync", sel_sb[:, :], sel[:, :], inc="sIn")
            plan.dma("sync", borow, bo.rearrange("(o f) -> o f", o=1),
                     inc="sIn")

            def qt_dma(hh):
                plan.dma("sync", QTs[:, hh * 2048:(hh + 1) * 2048],
                         QT[hh * 128:(hh + 1) * 128, :], inc="sIn")

            kv_guard = {}
            masked_cnt = {}
            state = {"cn": 0, "pend": None}

            def kv_dma(idx):
                h, rs = idx // 8, idx % 4
                pr = idx % 3
                w = [("gMisc", gm_all)] if idx < 3 else \
                    [("tPV", kv_guard[idx - 3])]
                plan.dma("sync", Krs[pr][:, :],
                         KTg[rs * 2048 + h * 128: rs * 2048 + (h + 1) * 128, :],
                         inc="sKV", waits=w)
                plan.dma("sync",
                         Vrs[pr][:, :].rearrange("p (k f) -> p k f", f=128),
                         Vg[rs * 2048:(rs + 1) * 2048,
                            h * 128:(h + 1) * 128].rearrange(
                             "(k s) f -> s k f", s=128),
                         inc="sKV", waits=w)

            def emit_pv(job):
                (h, half, rs, jj, mychunk, q0, cw, pr, first_of_half,
                 last_jj) = job
                hs = half * 1024
                masked = mychunk in masked_cnt
                pieces = []
                p0 = q0 + (128 if masked else 0)
                while p0 < q0 + cw:
                    pb = hs + ((p0 - hs) // 512 + 1) * 512
                    pieces.append((p0, min(pb, q0 + cw) - p0, False))
                    p0 = pb
                if masked:
                    pieces.append((q0, 128, True))
                if not pieces:
                    pieces = [(q0, cw, masked)]
                npc = len(pieces)
                for i, (pq, pw, is_masked) in enumerate(pieces):
                    bank = (pq - hs) // 512
                    w = []
                    if i == 0:
                        if first_of_half:
                            idx = h * 2 + half
                            if idx >= 1:
                                w.append(("nEvac", idx))
                        if not is_masked:
                            w.append(("nE", mychunk + 1))
                    if is_masked:
                        w.append(("nM", masked_cnt[mychunk]))
                    st_pv = (bank not in bank_started[0])
                    bank_started[0].add(bank)
                    plan.op("tensor",
                            lambda e, jj=jj, pq=pq, pw=pw, pr=pr, hs=hs,
                            q0=q0, mychunk=mychunk, st=st_pv: e.matmul(
                                accT[:, pq - hs: pq - hs + pw],
                                Vrs[pr][:, jj * 128:(jj + 1) * 128],
                                pT[mychunk % 4][:, pq - q0: pq - q0 + pw],
                                start=st, stop=False,
                                skip_group_check=True),
                            waits=w)
                    st_sm = (bank not in bank_started[1])
                    bank_started[1].add(bank)
                    plan.op("tensor",
                            lambda e, pq=pq, pw=pw, hs=hs, q0=q0,
                            mychunk=mychunk, st=st_sm: e.matmul(
                                sums[0:1, pq - hs: pq - hs + pw],
                                ones_s[:, :],
                                pT[mychunk % 4][:, pq - q0: pq - q0 + pw],
                                start=st, stop=False,
                                skip_group_check=True),
                            inc="tPV" if i == npc - 1 else None)

            def emit_epi(job):
                h, hs, hidx, vrec = job
                for k in range(2):
                    w = [("vRec", vrec)]
                    plan.op("tensor", lambda e, k=k: e.matmul(
                        sums[:, k * 512:(k + 1) * 512], ones1[:, :],
                        s_row[0:1, k * 512:(k + 1) * 512],
                        start=True, stop=True),
                        waits=w, inc="tBc")
                plan.op("scalar", lambda e: e.activation(
                    Bo[:, 0:1024], sums[:, :], AF.Ln),
                    waits=[("tBc", plan.n("tBc"))])
                plan.op("scalar", lambda e: e.activation(
                    bc_sb[:, :], Bo[:, 0:1024], AF.Exp, scale=-1.0),
                    inc="vBc")
                plan.op("vector", lambda e, h=h, hs=hs: e.tensor_mul(
                    attnT[:, h * 2048 + hs: h * 2048 + hs + 1024],
                    accT[:, :], bc_sb[:, :]),
                    waits=[("vBc", plan.n("vBc"))], inc="nEvac")

            bank_started = [set(), set()]
            qt_dma(0)
            qt_dma(1)
            for i in range(3):
                kv_dma(i)
            for hh in range(2, H):
                qt_dma(hh)
            tL_h = {}
            for h in range(H):
                for half in range(2):
                    hs = half * 1024
                    bank_started[0] = set()
                    bank_started[1] = set()
                    last_jj = {b: min(15, (hs + (b + 1) * 512 - 1) // 128)
                               for b in range(2)}
                    for rs in range(4):
                        it = (h * 2 + half) * 4 + rs
                        pr = it % 3
                        first_of_half = (rs == 0)
                        first_of_window = True
                        for jj in range(16):
                            q0g = max(hs, jj * 128)
                            qend = hs + 1024
                            if q0g >= qend:
                                continue
                            q0 = q0g
                            cw = qend - q0
                            mychunk = state["cn"]
                            for ls in range(0, cw, 512):
                                lw = min(512, cw - ls)
                                w = []
                                if ls == 0:
                                    if first_of_window:
                                        w.append(("sKV", 32 * (it + 1)))
                                    if it == (h * 2) * 4 and first_of_window:
                                        w.append(("sIn", 48 + 16 * (h + 1)))
                                    if mychunk >= 2:
                                        w.append(("nE", mychunk - 1))
                                plan.op("tensor",
                                        lambda e, h=h, jj=jj, q0=q0, ls=ls,
                                        lw=lw, pr=pr, mychunk=mychunk:
                                        e.matmul(
                                            lg[mychunk % 2][:, ls:ls + lw],
                                            Krs[pr][:, jj * 128:(jj + 1) * 128],
                                            QTs[:, h * 2048 + q0 + ls:
                                                h * 2048 + q0 + ls + lw],
                                            start=True, stop=True),
                                        waits=w,
                                        inc="tL" if ls + lw == cw else None)
                            if state["pend"] is not None:
                                emit_pv(state["pend"])
                                state["pend"] = None
                            if state.get("pend_epi") is not None:
                                emit_epi(state["pend_epi"])
                                state["pend_epi"] = None
                            w = [("tL", plan.n("tL"))]
                            if mychunk >= 4:
                                w.append(("tPV", mychunk - 3))
                            plan.op("scalar",
                                    lambda e, mychunk=mychunk, cw=cw:
                                    e.activation(
                                        pT[mychunk % 4][:, 0:cw],
                                        lg[mychunk % 2][:, 0:cw], AF.Exp,
                                        scale=SCALE),
                                    waits=w, inc="nE")
                            first_of_window = False
                            if q0 == jj * 128:
                                plan.op("vector",
                                        lambda e, mychunk=mychunk, rs=rs:
                                        e.tensor_mul(
                                            pT[mychunk % 4][:, 0:128],
                                            pT[mychunk % 4][:, 0:128],
                                            mask_sb[:, rs * 128:
                                                    (rs + 1) * 128]),
                                        waits=[("nE", mychunk + 1)],
                                        inc="nM")
                                masked_cnt[mychunk] = plan.n("nM")
                            state["pend"] = (h, half, rs, jj, mychunk,
                                             q0, cw, pr, first_of_half,
                                             last_jj)
                            first_of_half = False
                            state["cn"] += 1
                        kv_guard[it] = state["cn"]
                        if it + 3 < H * 2 * 4:
                            kv_dma(it + 3)
                    if state["pend"] is not None:
                        emit_pv(state["pend"])
                        state["pend"] = None
                    hidx = h * 2 + half
                    plan.op("vector", lambda e: e.tensor_copy(
                        s_row[:, :], sums[0:1, 0:1024]),
                        waits=[("tPV", plan.n("tPV"))], inc="vRec")
                    state["pend_epi"] = (h, hs, hidx, plan.n("vRec"))
                tL_h[h] = plan.n("tL")
            if state.get("pend_epi") is not None:
                emit_epi(state["pend_epi"])
                state["pend_epi"] = None
            flash_tL = plan.n("tL")
            flash_tPV = plan.n("tPV")
            flash_nE = plan.n("nE")
            plan.emit(nc)

        with ExitStack() as c2:
            psO = [c2.enter_context(nc.psum_tensor(f"psO{i}", [128, 512], F32))
                   for i in range(4)]
            psSel = c2.enter_context(nc.psum_tensor("psSel", [2, 2048], F32))
            plan = Plan(sems)
            gate = [("nEvac", 32), ("tL", flash_tL), ("tPV", flash_tPV),
                    ("nE", flash_nE)]

            for nn in range(4):
                w = [("sIn", plan.n("sIn")), ("gMisc", gm_all)]
                if nn == 0:
                    w += gate
                if nn >= 2:
                    w.append(("vOut", nn - 1))
                plan.op("tensor", lambda e, nn=nn: e.matmul(
                    psO[nn % 4][:, :], ones1[:, :],
                    out8[0][0:1, nn * 512:(nn + 1) * 512],
                    start=True, stop=True),
                    waits=w, inc="tO")
                plan.op("vector", lambda e, nn=nn: e.tensor_copy(
                    Bo[:, nn * 512:(nn + 1) * 512], psO[nn % 4][:, :]),
                    waits=[("tO", nn + 1)], inc="vOut")

            ptoks = []
            for c in range(8):
                w = [] if c < 2 else [("tSel", c - 1)]
                ptoks.append(plan.dma("sync", ptmp[c % 2][:, :],
                                      pall[c * 8:(c + 1) * 8, :],
                                      inc="sP2", waits=w))
            for c in range(8):
                for nn in range(4):
                    w = []
                    if nn == 0:
                        w = [ptoks[c]]
                        if c == 0:
                            w.append(("sIn", 48))
                    plan.op("tensor", lambda e, c=c, nn=nn: e.matmul(
                        psSel[:, nn * 512:(nn + 1) * 512], sel_sb[0:8, :],
                        ptmp[c % 2][0:8, nn * 512:(nn + 1) * 512],
                        start=(c == 0), stop=(c == 7)),
                        waits=w, inc="tSel" if nn == 3 else None)

            wos_sl = [wos, QTs[:, 16 * 512:32 * 512]]
            for f4 in range(4):
                if f4 == 0:
                    w = [("nEvac", 16)]
                elif f4 == 1:
                    w = [("nEvac", 24)]
                else:
                    w = [("tO", 4 + (f4 - 1) * 16)]
                src = wo.rearrange("(cc p) f -> p cc f", p=128)[
                    :, :, f4 * 512:(f4 + 1) * 512]
                cast = plan.dma("gpsimd", wos_sl[f4 % 2][:, :].rearrange(
                    "p (cc f) -> p cc f", f=512), src, inc="gWo", waits=w)
                for tt in range(NT):
                    gl = f4 * 16 + tt
                    gi = plan.n("tO")
                    par = gi % 4
                    for cc in range(16):
                        w = []
                        if cc == 0:
                            w = [cast, ("vOut", max(4, plan.n("vOut") - 3))]
                        plan.op("tensor",
                                lambda e, f4=f4, tt=tt, cc=cc, par=par:
                                e.matmul(
                                    psO[par][:, :],
                                    attnT[:, cc * 2048 + tt * 128:
                                          cc * 2048 + (tt + 1) * 128],
                                    wos_sl[f4 % 2][:, cc * 512:(cc + 1) * 512],
                                    start=(cc == 0), stop=(cc == 15)),
                                waits=w, inc="tO" if cc == 15 else None)
                    dxw = [("vOut", max(4, plan.n("vOut") - 1))] if gl >= 2 else []
                    dx = plan.dma("scalar", xt[gl % 2][:, :],
                                  x[tt * 128:(tt + 1) * 128,
                                    f4 * 512:(f4 + 1) * 512],
                                  inc="sX2", waits=dxw)
                    opar = gl % 2
                    w = [("tO", gi + 1), dx]
                    if gl >= 2:
                        w.append(("sO", plan.n("sO") - 16))
                    plan.op("vector", lambda e, par=par, opar=opar:
                            e.scalar_tensor_tensor(
                                outsb[opar][:, :], psO[par][:, :], 1.0,
                                xt[opar][:, :], op0=ALU.mult, op1=ALU.add),
                            waits=w)
                    plan.op("vector", lambda e, f4=f4, opar=opar:
                            e.tensor_add(outsb[opar][:, :], outsb[opar][:, :],
                                         Bo[:, f4 * 512:(f4 + 1) * 512]))
                    if tt == 0:
                        plan.op("vector", lambda e, f4=f4, opar=opar:
                                e.tensor_add(
                                    outsb[opar][0:2, :], outsb[opar][0:2, :],
                                    psSel[:, f4 * 512:(f4 + 1) * 512]),
                                waits=[("tSel", 8)])
                    plan.op("vector", lambda e: e.engine_nop(), inc="vOut")
                    plan.dma("sync", o[tt * 128:(tt + 1) * 128,
                                      f4 * 512:(f4 + 1) * 512],
                             outsb[opar][:, :], inc="sO",
                             waits=[("vOut", plan.n("vOut"))])
            plan.barrier("sync", [("sO", plan.n("sO"))])
            plan.emit(nc)
    return nc



_NC_CACHE = {}


def _get_nc(which):
    if which not in _NC_CACHE:
        _NC_CACHE[which] = build_l1() if which == "l1" else build_l2()
    return _NC_CACHE[which]


def _build_masks(rc):
    m = np.zeros((4, 128, 128), dtype=bf16)
    s = np.arange(128)[:, None]
    q = np.arange(128)[None, :]
    for rs in range(4):
        off = 1 if rc < rs else 0
        m[rs] = (q >= s + off).astype(bf16)
    return m


def kernel(**inputs):
    xfull = np.asarray(inputs["L0_attn_query"], dtype=np.float32)
    wq = np.asarray(inputs["L0_attn_w_q"], np.float32)
    bq = np.asarray(inputs["L0_attn_b_q"], np.float32)
    wk = np.asarray(inputs["L0_attn_w_k"], np.float32)
    bk = np.asarray(inputs["L0_attn_b_k"], np.float32)
    wv = np.asarray(inputs["L0_attn_w_v"], np.float32)
    bv = np.asarray(inputs["L0_attn_b_v"], np.float32)
    wo = np.asarray(inputs["L0_attn_w_o"], np.float32)
    bo = np.asarray(inputs["L0_attn_b_o"], np.float32)
    w1 = np.asarray(inputs["L0_ffn_w1"], np.float32)
    w3 = np.asarray(inputs["L0_ffn_w3"], np.float32)
    w2 = np.asarray(inputs["L0_ffn_w2"], np.float32)
    wr = np.asarray(inputs["L0_ffn_wr"], np.float32)

    nc1 = _get_nc("l1")
    in1 = []
    xs = []
    for c in range(8):
        b, r = c // 4, c % 4
        xc = np.ascontiguousarray(xfull[b, r::4, :])
        xs.append(xc)
        in1.append(dict(x=xc, wq=wq, bq=bq, wk=wk, bk=bk, wv=wv, bv=bv,
                        w1=w1, w3=w3, w2=w2, wr=wr))
    r1 = run_bass_kernel_spmd(nc1, in1, core_ids=list(range(8)))

    KTs = [np.asarray(r1.results[c]["KT"]) for c in range(8)]
    Vs = [np.asarray(r1.results[c]["V"]) for c in range(8)]
    QTs = [np.asarray(r1.results[c]["QT"]) for c in range(8)]
    ps = [np.asarray(r1.results[c]["p"], np.float32) for c in range(8)]

    pall = np.ascontiguousarray(np.stack(ps, axis=0).reshape(64, D))
    nc2 = _get_nc("l2")
    in2 = []
    for c in range(8):
        b, r = c // 4, c % 4
        KTgc = np.ascontiguousarray(
            np.concatenate(KTs[b * 4:(b + 1) * 4], axis=0))
        Vgc = np.ascontiguousarray(
            np.concatenate(Vs[b * 4:(b + 1) * 4], axis=0))
        selm = np.zeros((8, 2), np.float32)
        if b == 0:
            selm[r, 0] = 1.0
            selm[r + 4, 1] = 1.0
        in2.append(dict(KTg=KTgc, Vg=Vgc, QT=QTs[c], x=xs[c], wo=wo, bo=bo,
                        masks=_build_masks(r), sel=selm, pall=pall))
    r2 = run_bass_kernel_spmd(nc2, in2, core_ids=list(range(8)))

    out = np.empty((2, 8192, D), np.float32)
    for c in range(8):
        b, r = c // 4, c % 4
        out[b, r::4, :] = np.asarray(r2.results[c]["o"], np.float32)
    return out


# version 30
# speedup vs baseline: 1.0176x; 1.0003x over previous
import numpy as np
import ml_dtypes
from contextlib import ExitStack

import concourse.bass as bass
import concourse.mybir as mybir
from concourse.bass_utils import run_bass_kernel_spmd

F32 = mybir.dt.float32
BF16 = mybir.dt.bfloat16
AF = mybir.ActivationFunctionType
ALU = mybir.AluOpType
bf16 = ml_dtypes.bfloat16

D = 2048
H = 16
HS = 128
NT = 16
DC = 16
EPS = 1e-5
SCALE = float(1.0 / np.sqrt(np.float32(HS)))

ENGINES = ("sync", "gpsimd", "tensor", "vector", "scalar")
DEBUG = False


class Plan:

    def __init__(self, sems):
        self.ops = {e: [] for e in ENGINES}
        self.sems = sems
        self.last_tok = sems.setdefault("__last_tok", {})

    def n(self, name):
        return self.sems[name][1]

    def op(self, eng, fn, waits=(), inc=None, inc_by=1, is_dma=False):
        if inc is not None:
            self.sems[inc][1] += inc_by
        self.ops[eng].append((fn, tuple(waits), inc, inc_by, is_dma))
        return (inc, self.sems[inc][1]) if inc is not None else None

    def dma(self, eng, out, in_, inc, waits=(), **kw):
        def fn(e, o=out, i=in_, kw=kw):
            return e.dma_start(out=o, in_=i, **kw)
        return self.op(eng, fn, waits, inc, 16, is_dma=True)

    def barrier(self, eng, waits):
        self.ops[eng].append((None, tuple(waits), None, 0, False))

    def emit(self, nc):
        with nc.Block() as block:
            for eng in ENGINES:
                ops = self.ops[eng]
                if not ops:
                    continue
                selfs = ("__self_" + eng) if eng in ("vector", "scalar") else None

                def body(e, _ops=ops, _selfs=selfs, _eng=eng):
                    sh = self.sems[_selfs] if _selfs else None
                    last = self.last_tok.get(_eng)
                    for fn, waits, inc, inc_by, is_dma in _ops:
                        for (sname, v) in waits:
                            e.wait_ge(self.sems[sname][0], v)
                        if fn is None:
                            continue
                        if sh is not None and last is not None and not is_dma:
                            e.wait_ge(last[0], last[1])
                        ins = fn(e)
                        if inc is not None:
                            s = self.sems[inc]
                            s[2] += inc_by
                            ins.then_inc(s[0], inc_by)
                            if sh is not None and not is_dma:
                                last = (s[0], s[2])
                        elif sh is not None and not is_dma:
                            sh[2] += 1
                            ins.then_inc(sh[0], 1)
                            last = (sh[0], sh[2])
                    if _selfs:
                        self.last_tok[_eng] = last
                getattr(block, eng)(body)


def _mk_sems(ctx, nc, names):
    names = list(names) + ["__self_vector", "__self_scalar"]
    return {n: [ctx.enter_context(nc.semaphore(n)), 0, 0] for n in names}


def _ident(plan, ident, sem):
    plan.op("gpsimd", lambda e: e.memset(ident[:, :], 0.0), inc=sem)
    plan.op("gpsimd", lambda e: e.affine_select(
        out=ident[:, :], in_=ident[:, :], compare_op=ALU.not_equal,
        fill=1.0, base=0, pattern=[[-1, 128]], channel_multiplier=1),
        inc=sem)



def build_l1():
    nc = bass.Bass("TRN2", target_bir_lowering=False, debug=False)
    x = nc.dram_tensor("x", [2048, D], F32, kind="ExternalInput").ap()
    wq = nc.dram_tensor("wq", [D, D], F32, kind="ExternalInput").ap()
    bq = nc.dram_tensor("bq", [D], F32, kind="ExternalInput").ap()
    wk = nc.dram_tensor("wk", [D, D], F32, kind="ExternalInput").ap()
    bk = nc.dram_tensor("bk", [D], F32, kind="ExternalInput").ap()
    wv = nc.dram_tensor("wv", [D, D], F32, kind="ExternalInput").ap()
    bv = nc.dram_tensor("bv", [D], F32, kind="ExternalInput").ap()
    w1 = nc.dram_tensor("w1", [D, D], F32, kind="ExternalInput").ap()
    w3 = nc.dram_tensor("w3", [D, D], F32, kind="ExternalInput").ap()
    w2 = nc.dram_tensor("w2", [D, D], F32, kind="ExternalInput").ap()
    wr = nc.dram_tensor("wr", [8, D], F32, kind="ExternalInput").ap()
    KT_o = nc.dram_tensor("KT", [D, 2048], BF16, kind="ExternalOutput").ap()
    V_o = nc.dram_tensor("V", [2048, D], BF16, kind="ExternalOutput").ap()
    QT_o = nc.dram_tensor("QT", [D, 2048], BF16, kind="ExternalOutput").ap()
    p_o = nc.dram_tensor("p", [8, D], F32, kind="ExternalOutput").ap()
    if DEBUG:
        dbg_xnT = nc.dram_tensor("dbg_xnT", [128, DC * 2048], BF16,
                                 kind="ExternalOutput").ap()
        dbg_W = nc.dram_tensor("dbg_W", [128, 128], BF16,
                               kind="ExternalOutput").ap()
        dbg_h1 = nc.dram_tensor("dbg_h1", [128, 2048], BF16,
                                kind="ExternalOutput").ap()
        dbg_h2 = nc.dram_tensor("dbg_h2", [128, 2048], BF16,
                                kind="ExternalOutput").ap()

    with ExitStack() as ctx:
        E = ctx.enter_context
        xnT = E(nc.sbuf_tensor("xnT", [128, DC * 2048], BF16))
        h1T = E(nc.sbuf_tensor("h1T", [128, 16 * 2048], BF16))
        wslab = [E(nc.sbuf_tensor(f"wslab{i}", [128, 2048], BF16)) for i in range(3)]
        wtsl = [E(nc.sbuf_tensor(f"wtsl{i}", [128, 2048], BF16)) for i in range(2)]
        stage = [E(nc.sbuf_tensor(f"stage{i}", [128, 2048], BF16)) for i in range(2)]
        h3t = [E(nc.sbuf_tensor(f"h3t{i}", [128, 512], BF16)) for i in range(2)]
        xt = [E(nc.sbuf_tensor("xt0", [128, 2048], F32)), None]
        xnb = [E(nc.sbuf_tensor(f"xnb{i}", [128, 2048], BF16)) for i in range(2)]
        xcen = E(nc.sbuf_tensor("xcen", [128, 2048], F32))
        ident = E(nc.sbuf_tensor("ident", [128, 128], BF16))
        eps_t = E(nc.sbuf_tensor("eps_t", [128, 1], F32))
        bqb = E(nc.sbuf_tensor("bqb", [128, 16], F32))
        bkb = E(nc.sbuf_tensor("bkb", [128, 16], F32))
        Bv = E(nc.sbuf_tensor("Bv", [128, 2048], F32))
        xt[1] = Bv
        ones1 = E(nc.sbuf_tensor("ones1", [1, 128], F32))
        wr_bf = E(nc.sbuf_tensor("wr_bf", [8, 2048], BF16))
        wrT = E(nc.sbuf_tensor("wrT", [128, 16 * 8], BF16))
        Wsb = E(nc.sbuf_tensor("Wsb", [128, 16 * 8], BF16))
        out8 = E(nc.sbuf_tensor("out8", [8, 2048], F32))
        bvrow = out8[0:1, :]
        s1 = [E(nc.sbuf_tensor(f"s1_{i}", [128, 1], F32)) for i in range(2)]
        mmean = [E(nc.sbuf_tensor(f"m_{i}", [128, 1], F32)) for i in range(2)]
        m2t = [E(nc.sbuf_tensor(f"m2_{i}", [128, 1], F32)) for i in range(2)]
        vvar = [E(nc.sbuf_tensor(f"vv_{i}", [128, 1], F32)) for i in range(2)]
        sacc = [E(nc.sbuf_tensor(f"sacc_{i}", [128, 1], F32)) for i in range(2)]
        rstd = [E(nc.sbuf_tensor(f"rstd_{i}", [128, 1], F32)) for i in range(2)]
        r_eq1 = E(nc.sbuf_tensor("r_eq1", [128, 8], F32))
        r_eq2 = E(nc.sbuf_tensor("r_eq2", [128, 8], F32))
        r_l2 = E(nc.sbuf_tensor("r_l2", [128, 8], F32))
        r_t1 = E(nc.sbuf_tensor("r_t1", [128, 8], F32))
        r_m1 = E(nc.sbuf_tensor("r_m1", [128, 1], F32))
        r_m2 = E(nc.sbuf_tensor("r_m2", [128, 1], F32))
        r_d = E(nc.sbuf_tensor("r_d", [128, 1], F32))
        r_g1 = E(nc.sbuf_tensor("r_g1", [128, 1], F32))
        r_g2 = E(nc.sbuf_tensor("r_g2", [128, 1], F32))

        sems = _mk_sems(ctx, nc, [
            "sX", "eSq", "vStat", "eRstd", "vXN", "vMn", "eXc", "tTr",
            "vTr", "gMisc",
            "sB", "gK", "tK", "eK", "sKout", "gV", "tV", "vV", "sV",
            "tBv", "vBv", "gWr", "tWr", "vWrT", "tR", "vRm", "eRg", "vRt",
            "gF", "tFtr", "vFtr", "tH", "eH", "vH3", "tS", "vO8", "sP",
        ])

        with ExitStack() as c2:
            psA = [c2.enter_context(nc.psum_tensor(f"psA{i}", [128, 128], BF16))
                   for i in range(2)]
            psB = [c2.enter_context(nc.psum_tensor(f"psB{i}", [128, 512], F32))
                   for i in range(4)]
            psR = c2.enter_context(nc.psum_tensor("psR", [128, 8], F32))
            psTB = psA
            plan = Plan(sems)
            _ident(plan, ident, "gMisc")
            plan.op("gpsimd", lambda e: e.memset(ones1[:, :], 1.0), inc="gMisc")
            plan.op("gpsimd", lambda e: e.memset(eps_t[:, :], EPS), inc="gMisc")
            for (bsb, bsrc) in ((bkb, bk), (bqb, bq)):
                for c in range(16):
                    plan.dma("scalar", bsb[:, c:c + 1],
                             bsrc[c * 128:(c + 1) * 128].rearrange(
                                 "(p o) -> p o", o=1), inc="sB")
            plan.dma("scalar", bvrow,
                     bv.rearrange("(o f) -> o f", o=1), inc="sB")

            gtr = 0
            for tt in range(NT):
                p = tt % 2
                waits = [("eSq", tt - 1),
                         ("vXN", tt - 1)] if tt >= 2 else []
                dx = plan.dma("sync" if tt % 2 == 0 else "scalar",
                              xt[p][:, :],
                              x[tt * 128:(tt + 1) * 128, :], inc="sX",
                              waits=waits)
                plan.op("vector", lambda e, p=p: e.reduce_sum(
                    s1[p][:, :], xt[p][:, :], axis=mybir.AxisListType.X),
                    waits=[dx])
                plan.op("vector", lambda e, p=p: e.tensor_scalar_mul(
                    mmean[p][:, :], s1[p][:, :], -1.0 / 2048.0), inc="vMn")
                sq_w = [dx] + ([("vStat", tt - 1)] if tt >= 2 else [])
                sq = plan.op("scalar", lambda e, p=p: e.activation(
                    stage[0][:, :], xt[p][:, :], AF.Square,
                    accum_out=sacc[p][:, :]), waits=sq_w, inc="eSq")
                plan.op("vector", lambda e, p=p: e.tensor_scalar_mul(
                    s1[p][:, :], sacc[p][:, :], 1.0 / 2048.0), waits=[sq])
                plan.op("vector", lambda e, p=p: e.tensor_mul(
                    m2t[p][:, :], mmean[p][:, :], mmean[p][:, :]))
                st = plan.op("vector", lambda e, p=p: e.tensor_sub(
                    vvar[p][:, :], s1[p][:, :], m2t[p][:, :]), inc="vStat")
                ln_w = [st] + ([("gMisc", 4)] if tt == 0 else [])
                plan.op("scalar", lambda e, p=p: e.activation(
                    m2t[p][:, :], vvar[p][:, :], AF.Ln, bias=eps_t[:, :]),
                    waits=ln_w)
                plan.op("scalar", lambda e, p=p: e.activation(
                    rstd[p][:, :], m2t[p][:, :], AF.Exp, scale=-0.5),
                    inc="eRstd")
                xc_w = [("vMn", tt + 1)]
                if tt >= 1:
                    xc_w.append(("vXN", tt))
                xc = plan.op("scalar", lambda e, p=p: e.activation(
                    xcen[:, :], xt[p][:, :], AF.Identity,
                    bias=mmean[p][:, :]), waits=xc_w, inc="eXc")
                xn_w = [xc, ("eRstd", tt + 1)] +                     ([("tTr", (tt - 2) * 16 + 16)] if tt >= 2 else [])
                plan.op("vector", lambda e, p=p: e.tensor_scalar_mul(
                    xnb[p][:, :], xcen[:, :], rstd[p][:, :]),
                    waits=xn_w, inc="vXN")
                for dc in range(DC):
                    g = gtr
                    w = [("vXN", tt + 1), ("gMisc", 3)]
                    if g >= 2:
                        w.append(("vTr", g - 1))
                    plan.op("tensor", lambda e, p=p, dc=dc, g=g: e.transpose(
                        psA[g % 2][:, :], xnb[p][:, dc * 128:(dc + 1) * 128],
                        ident[:, :]), waits=w, inc="tTr")
                    plan.op("vector", lambda e, tt=tt, dc=dc, g=g: e.tensor_copy(
                        xnT[:, dc * 2048 + tt * 128: dc * 2048 + (tt + 1) * 128],
                        psA[g % 2][:, :]), waits=[("tTr", g + 1)], inc="vTr")
                    gtr += 1


            def kq_phase(wsrc, bsb, out_ap, mm_gate, cast_guard,
                         rdy=lambda fc, t4: []):
                base_tK = plan.n("tK")
                base_sK = plan.n("sKout")
                for fc in range(16):
                    if fc >= 2:
                        cw = [("tK", base_tK + 4 * (fc - 1))]
                    else:
                        cw = list(cast_guard)
                    src = wsrc.rearrange("(dc p) f -> p dc f", p=128)[
                        :, :, fc * 128:(fc + 1) * 128]
                    dst = wslab[fc % 2][:, :].rearrange(
                        "p (dc f) -> p dc f", f=128)
                    cast = plan.dma("gpsimd", dst, src, inc="gK", waits=cw)
                    for tp in range(2):
                        t4a, t4b = 2 * tp, 2 * tp + 1
                        gi = plan.n("tK")
                        pb = ((gi // 2) % 2) * 2
                        for dc in range(DC):
                            for sl, t4 in ((0, t4a), (1, t4b)):
                                w = []
                                if dc == 0 and sl == 0:
                                    w = [cast, ("eK", max(0, gi - 2))]
                                    if fc == 0 and tp == 0:
                                        w += mm_gate
                                    w += rdy(fc, t4b)
                                plan.op("tensor",
                                        lambda e, fc=fc, t4=t4, dc=dc,
                                        bk=pb + sl: e.matmul(
                                            psB[bk][:, :],
                                            wslab[fc % 2][:, dc * 128:
                                                          (dc + 1) * 128],
                                            xnT[:, dc * 2048 + t4 * 512:
                                                dc * 2048 + (t4 + 1) * 512],
                                            start=(dc == 0),
                                            stop=(dc == DC - 1)),
                                        waits=w,
                                        inc="tK" if dc == DC - 1 else None)
                        for off, t4 in ((1, t4a), (2, t4b)):
                            w = [("tK", gi + off)]
                            if fc == 0 and tp == 0 and off == 1 \
                                    and base_sK == 0:
                                w.append(("sB", 528))
                            if tp == 0 and off == 1:
                                if fc >= 2:
                                    w.append(("sKout", base_sK + 16 * (fc - 1)))
                                elif base_sK > 0:
                                    w.append(("sKout", base_sK))
                            plan.op("scalar",
                                    lambda e, fc=fc, t4=t4, bk=pb + off - 1,
                                    bsb=bsb: e.activation(
                                        stage[fc % 2][:, t4 * 512:
                                                      (t4 + 1) * 512],
                                        psB[bk][:, :], AF.Identity,
                                        bias=bsb[:, fc:fc + 1]),
                                    waits=w, inc="eK")
                    plan.dma("sync", out_ap[fc * 128:(fc + 1) * 128, :],
                             stage[fc % 2][:, :], inc="sKout",
                             waits=[("eK", plan.n("eK"))])

            if DEBUG:
                plan.dma("sync", dbg_xnT[:, :], xnT[:, :], inc="sP",
                         waits=[("vTr", 256)])
            kq_phase(wk, bkb, KT_o, [], [],
                     rdy=lambda fc, t4: (
                         [("vTr", (4 * t4 + 4) * 16)] if fc == 0 else []))

            plan.dma("gpsimd", wr_bf[:, :], wr[:, :], inc="gWr",
                     waits=[("vXN", 16)])
            for dc in range(DC):
                w = [("gWr", 16), ("gMisc", 3)]
                if dc == 0:
                    w.append(("vTr", 256))
                if dc >= 2:
                    w.append(("vWrT", dc - 1))
                plan.op("tensor", lambda e, dc=dc: e.transpose(
                    psTB[dc % 2][:, 0:8], wr_bf[:, dc * 128:(dc + 1) * 128],
                    ident[0:8, 0:8]), waits=w, inc="tWr")
                plan.op("vector", lambda e, dc=dc: e.tensor_copy(
                    wrT[:, dc * 8:(dc + 1) * 8], psTB[dc % 2][:, 0:8]),
                    waits=[("tWr", dc + 1)], inc="vWrT")
            for tt in range(NT):
                for dc in range(DC):
                    w = []
                    if dc == 0:
                        w = [("vWrT", 16)]
                        if tt:
                            w.append(("vRt", tt))
                    plan.op("tensor", lambda e, tt=tt, dc=dc: e.matmul(
                        psR[:, :],
                        xnT[:, dc * 2048 + tt * 128: dc * 2048 + (tt + 1) * 128],
                        wrT[:, dc * 8:(dc + 1) * 8],
                        start=(dc == 0), stop=(dc == DC - 1)),
                        waits=w, inc="tR" if dc == DC - 1 else None)
                plan.op("vector", lambda e, tt=tt: e.reduce_max(
                    r_m1[:, :], psR[:, :], axis=mybir.AxisListType.X),
                    waits=[("tR", tt + 1)])
                plan.op("vector", lambda e: e.tensor_single_scalar(
                    r_eq1[:, :], psR[:, :], r_m1[:, :], ALU.is_equal))
                plan.op("vector", lambda e: e.scalar_tensor_tensor(
                    r_l2[:, :], r_eq1[:, :], -1e30, psR[:, :],
                    op0=ALU.mult, op1=ALU.add))
                plan.op("vector", lambda e: e.reduce_max(
                    r_m2[:, :], r_l2[:, :], axis=mybir.AxisListType.X))
                plan.op("vector", lambda e: e.tensor_single_scalar(
                    r_eq2[:, :], r_l2[:, :], r_m2[:, :], ALU.is_equal))
                plan.op("vector", lambda e: e.tensor_sub(
                    r_d[:, :], r_m1[:, :], r_m2[:, :]), inc="vRm")
                plan.op("scalar", lambda e, tt=tt: e.activation(
                    r_g1[:, :], r_d[:, :], AF.Sigmoid),
                    waits=[("vRm", tt + 1)])
                plan.op("scalar", lambda e: e.activation(
                    r_g2[:, :], r_d[:, :], AF.Sigmoid, scale=-1.0), inc="eRg")
                plan.op("vector", lambda e: e.tensor_scalar_mul(
                    r_t1[:, :], r_eq1[:, :], r_g1[:, :]),
                    waits=[("eRg", tt + 1)])
                plan.op("vector", lambda e, tt=tt: e.scalar_tensor_tensor(
                    Wsb[:, tt * 8:(tt + 1) * 8], r_eq2[:, :], r_g2[:, :],
                    r_t1[:, :], op0=ALU.mult, op1=ALU.add), inc="vRt")

            kq_phase(wq, bqb, QT_o, [], [("tK", 64)])

            for nn in range(4):
                w = [("gMisc", 3), ("sB", plan.n("sB")),
                     ("eK", plan.n("eK"))]
                if nn >= 2:
                    w.append(("vBv", nn - 1))
                plan.op("tensor", lambda e, nn=nn: e.matmul(
                    psB[nn % 2][:, :], ones1[:, :],
                    bvrow[:, nn * 512:(nn + 1) * 512], start=True, stop=True),
                    waits=w, inc="tBv")
                plan.op("vector", lambda e, nn=nn: e.tensor_copy(
                    Bv[:, nn * 512:(nn + 1) * 512], psB[nn % 2][:, :]),
                    waits=[("tBv", nn + 1)], inc="vBv")

            wvslab = [h1T[:, 0:16 * 512], h1T[:, 16 * 512:32 * 512]]
            for f4 in range(4):
                if f4 >= 2:
                    w = [("tV", 16 * (f4 - 1))]
                else:
                    w = [("vTr", 256)]
                src = wv.rearrange("(dc p) f -> p dc f", p=128)[
                    :, :, f4 * 512:(f4 + 1) * 512]
                dst = wvslab[f4 % 2].rearrange("p (dc f) -> p dc f", f=512)
                cast = plan.dma("gpsimd", dst, src, inc="gV", waits=w)
                for tt in range(NT):
                    gi = plan.n("tV")
                    par = gi % 2
                    for dc in range(DC):
                        w = []
                        if dc == 0:
                            w = [cast, ("vV", max(0, gi - 1))]
                            if gi == 0:
                                w.append(("eK", 128))
                        plan.op("tensor",
                                lambda e, f4=f4, tt=tt, dc=dc, par=par:
                                e.matmul(
                                    psB[par][:, :],
                                    xnT[:, dc * 2048 + tt * 128:
                                        dc * 2048 + (tt + 1) * 128],
                                    wvslab[f4 % 2][:, dc * 512:(dc + 1) * 512],
                                    start=(dc == 0), stop=(dc == DC - 1)),
                                waits=w, inc="tV" if dc == DC - 1 else None)
                    w = [("tV", gi + 1), ("vBv", 4)]
                    if gi >= 2:
                        w.append(("sV", 16 * (gi - 1)))
                    plan.op("vector", lambda e, f4=f4, tt=tt, par=par:
                            e.scalar_tensor_tensor(
                                h3t[gi % 2][:, :],
                                psB[par][:, :], 1.0,
                                Bv[:, f4 * 512:(f4 + 1) * 512],
                                op0=ALU.mult, op1=ALU.add),
                            waits=w, inc="vV")
                    plan.dma("sync",
                             V_o[tt * 128:(tt + 1) * 128,
                                 f4 * 512:(f4 + 1) * 512],
                             h3t[gi % 2][:, :], inc="sV",
                             waits=[("vV", plan.n("vV"))])


            plan.emit(nc)

        with ExitStack() as c2:
            psC = [c2.enter_context(nc.psum_tensor(f"psC{i}", [128, 512], F32))
                   for i in range(4)]
            psT = [c2.enter_context(nc.psum_tensor(f"psT{i}", [128, 128], BF16))
                   for i in range(4)]
            plan = Plan(sems)

            def hstage(wsrc, in_sb, out_sb, act, mm_gate, cast_guard,
                       tr_gate=()):
                base_tFtr = plan.n("tFtr")
                base_tH = plan.n("tH")
                for ic in range(16):
                    if ic >= 3:
                        cw = [("tFtr", base_tFtr + 16 * (ic - 2))]
                    else:
                        cw = list(cast_guard[min(ic, len(cast_guard) - 1)])
                    cast = plan.dma("gpsimd", wslab[ic % 3][:, :],
                                    wsrc[ic * 128:(ic + 1) * 128, :],
                                    inc="gF", waits=cw)
                    for dc in range(DC):
                        g = plan.n("tFtr")
                        w = [cast]
                        if ic == 0 and dc == 0:
                            w += list(tr_gate)
                        if g >= 4:
                            w.append(("vFtr", g - 3))
                        plan.op("tensor", lambda e, ic=ic, dc=dc, g=g:
                                e.transpose(
                                    psT[g % 4][:, :],
                                    wslab[ic % 3][:, dc * 128:(dc + 1) * 128],
                                    ident[:, :]), waits=w, inc="tFtr")
                        evw = [("tFtr", g + 1)]
                        if ic == 0 and dc == 0:
                            evw += mm_gate
                        if dc == 0:
                            if ic >= 2:
                                evw.append(("tH", base_tH + 4 * (ic - 1)))
                            elif base_tH > 0:
                                evw.append(("tH", base_tH))
                        plan.op("vector", lambda e, ic=ic, dc=dc, g=g:
                                e.tensor_copy(
                                    wtsl[ic % 2][:, dc * 128:(dc + 1) * 128],
                                    psT[g % 4][:, :]), waits=evw, inc="vFtr")
                    for tp in range(2):
                        t4a, t4b = 2 * tp, 2 * tp + 1
                        gi = plan.n("tH")
                        pb = ((gi // 2) % 2) * 2
                        for dc in range(DC):
                            for sl, t4 in ((0, t4a), (1, t4b)):
                                w = []
                                if dc == 0 and sl == 0:
                                    w = [("vFtr", plan.n("vFtr")),
                                         ("eH", max(0, gi - 2))]
                                plan.op("tensor",
                                        lambda e, ic=ic, t4=t4, dc=dc,
                                        bk=pb + sl: e.matmul(
                                            psC[bk][:, :],
                                            wtsl[ic % 2][:, dc * 128:
                                                         (dc + 1) * 128],
                                            in_sb[:, dc * 2048 + t4 * 512:
                                                  dc * 2048 + (t4 + 1) * 512],
                                            start=(dc == 0),
                                            stop=(dc == DC - 1)),
                                        waits=w,
                                        inc="tH" if dc == DC - 1 else None)
                        for off, t4 in ((1, t4a), (2, t4b)):
                            plan.op("scalar",
                                    lambda e, ic=ic, t4=t4, bk=pb + off - 1:
                                    e.activation(
                                        out_sb[:, ic * 2048 + t4 * 512:
                                               ic * 2048 + (t4 + 1) * 512],
                                        psC[bk][:, :], act),
                                    waits=[("tH", gi + off)], inc="eH")

            if DEBUG:
                plan.dma("sync", dbg_W[:, :], Wsb[:, :], inc="sP",
                         waits=[("vRt", 16)])
            gateC = [("eK", plan.n("eK")), ("vV", plan.n("vV")),
                     ("vRt", 16), ("sV", plan.n("sV"))]
            hstage(w1, xnT, h1T, AF.Silu, gateC,
                   [[("tK", 124)], [("tK", 128)], []],
                   tr_gate=[("eK", plan.n("eK")), ("vRt", 16),
                            ("vV", plan.n("vV"))])
            base_h2_eH = plan.n("eH")
            hstage(w3, h1T, xnT, AF.Silu,
                   [("tH", plan.n("tH")), ("tR", 16)],
                   [[("tFtr", 256)]])

            if DEBUG:
                plan.dma("sync", dbg_h1[:, :], h1T[:, 0:2048], inc="sP",
                         waits=[("eH", 64)])
                plan.dma("sync", dbg_h2[:, :], xnT[:, 0:2048], inc="sP",
                         waits=[("eH", 128)])
            plan.emit(nc)

        with ExitStack() as c3:
            psC = [c3.enter_context(nc.psum_tensor(f"psC3{i}", [128, 512],
                                                   F32)) for i in range(2)]
            psT = [c3.enter_context(nc.psum_tensor(f"psT3{i}", [128, 128],
                                                   BF16)) for i in range(2)]
            ps8 = c3.enter_context(nc.psum_tensor("ps8", [8, 2048], F32))
            plan = Plan(sems)
            h2T = xnT
            w2Tsl = [h1T[:, 0:16 * 512], h1T[:, 16 * 512:32 * 512]]
            base3_tH = plan.n("tH")
            base3_tFtr = plan.n("tFtr")
            eH_all = plan.n("eH")
            first_scatter = True
            for f4 in range(4):
                for k in range(4):
                    si3 = f4 * 4 + k
                    if si3 >= 2:
                        cw = [("tFtr", base3_tFtr + 16 * (si3 - 1))]
                    else:
                        cw = [("tFtr", base3_tFtr)]
                    cast = plan.dma("gpsimd", wslab[k % 2][:, :],
                                    w2[f4 * 512 + k * 128:
                                       f4 * 512 + (k + 1) * 128, :],
                                    inc="gF", waits=cw)
                    for ic in range(16):
                        g = plan.n("tFtr")
                        w = [cast]
                        if g == base3_tFtr:
                            w.append(("eH", eH_all))
                        if g >= 2:
                            w.append(("vFtr", g - 1))
                        plan.op("tensor", lambda e, k=k, ic=ic, g=g:
                                e.transpose(
                                    psT[g % 2][:, :],
                                    wslab[k % 2][:, ic * 128:(ic + 1) * 128],
                                    ident[:, :]), waits=w, inc="tFtr")
                        evw = [("tFtr", g + 1)]
                        if si3 == 0 and ic == 0:
                            evw.append(("tH", base3_tH))
                        if si3 >= 8 and ic == 0:
                            evw.append(("tH", base3_tH + (f4 - 1) * 16))
                        plan.op("vector", lambda e, f4=f4, k=k, ic=ic, g=g:
                                e.tensor_copy(
                                    w2Tsl[f4 % 2][:, ic * 512 + k * 128:
                                                  ic * 512 + (k + 1) * 128],
                                    psT[g % 2][:, :]), waits=evw, inc="vFtr")
                for tt in range(NT):
                    gl = f4 * 16 + tt
                    par = gl % 2
                    for ic in range(16):
                        w = []
                        if ic == 0:
                            w = [("vFtr", plan.n("vFtr"))]
                            if gl < 2:
                                w.append(("eH", eH_all))
                            else:
                                w.append(("vH3", gl - 1))
                        plan.op("tensor",
                                lambda e, f4=f4, tt=tt, ic=ic, par=par:
                                e.matmul(
                                    psC[par][:, :],
                                    h2T[:, ic * 2048 + tt * 128:
                                        ic * 2048 + (tt + 1) * 128],
                                    w2Tsl[f4 % 2][:, ic * 512:(ic + 1) * 512],
                                    start=(ic == 0), stop=(ic == 15)),
                                waits=w, inc="tH" if ic == 15 else None)
                    if tt > 0:
                        w = [("vH3", plan.n("vH3"))]
                        if first_scatter:
                            w.append(("vRt", 16))
                            first_scatter = False
                        plan.op("tensor", lambda e, f4=f4, tt=tt, gl=gl:
                                e.matmul(
                                    ps8[0:8, f4 * 512:(f4 + 1) * 512],
                                    Wsb[:, (tt - 1) * 8: tt * 8],
                                    h3t[(gl - 1) % 2][:, :],
                                    start=(tt - 1 == 0), stop=False),
                                waits=w, inc="tS")
                    w = [("tH", base3_tH + gl + 1)]
                    if gl >= 2:
                        w.append(("tS", plan.n("tS")))
                    plan.op("vector", lambda e, par=par: e.tensor_copy(
                        h3t[par][:, :], psC[par][:, :]), waits=w, inc="vH3")
                plan.op("tensor", lambda e, f4=f4: e.matmul(
                    ps8[0:8, f4 * 512:(f4 + 1) * 512],
                    Wsb[:, 15 * 8:16 * 8],
                    h3t[(f4 * 16 + 15) % 2][:, :],
                    start=False, stop=True),
                    waits=[("vH3", plan.n("vH3"))], inc="tS")
            plan.op("vector", lambda e: e.tensor_copy(out8[:, :], ps8[0:8, :]),
                    waits=[("tS", plan.n("tS"))], inc="vO8")
            plan.dma("sync", p_o[:, :], out8[:, :], inc="sP",
                     waits=[("vO8", 1)])
            plan.barrier("sync", [("sKout", plan.n("sKout")),
                                  ("sV", plan.n("sV")), ("sP", plan.n("sP"))])
            plan.emit(nc)
    return nc



def build_l2():
    nc = bass.Bass("TRN2", target_bir_lowering=False, debug=False)
    KTg = nc.dram_tensor("KTg", [4 * D, 2048], BF16, kind="ExternalInput").ap()
    Vg = nc.dram_tensor("Vg", [4 * 2048, D], BF16, kind="ExternalInput").ap()
    QT = nc.dram_tensor("QT", [D, 2048], BF16, kind="ExternalInput").ap()
    x = nc.dram_tensor("x", [2048, D], F32, kind="ExternalInput").ap()
    wo = nc.dram_tensor("wo", [D, D], F32, kind="ExternalInput").ap()
    bo = nc.dram_tensor("bo", [D], F32, kind="ExternalInput").ap()
    masks = nc.dram_tensor("masks", [4, 128, 128], BF16,
                           kind="ExternalInput").ap()
    sel = nc.dram_tensor("sel", [8, 2], F32, kind="ExternalInput").ap()
    pall = nc.dram_tensor("pall", [64, D], F32, kind="ExternalInput").ap()
    o = nc.dram_tensor("o", [2048, D], F32, kind="ExternalOutput").ap()

    with ExitStack() as ctx:
        E = ctx.enter_context
        QTs = E(nc.sbuf_tensor("QTs", [128, H * 2048], BF16))
        attnT = E(nc.sbuf_tensor("attnT", [128, H * 2048], BF16))
        Krs = [E(nc.sbuf_tensor(f"Krs{i}", [128, 2048], BF16)) for i in range(3)]
        Vrs = [E(nc.sbuf_tensor(f"Vrs{i}", [128, 2048], BF16)) for i in range(3)]
        pT = [E(nc.sbuf_tensor(f"pT{i}", [128, 1024], BF16)) for i in range(4)]
        mask_sb = E(nc.sbuf_tensor("mask_sb", [128, 4 * 128], BF16))
        xt = [E(nc.sbuf_tensor(f"xtL2{i}", [128, 512], F32)) for i in range(2)]
        Bo = E(nc.sbuf_tensor("Bo", [128, 2048], F32))
        outsb = [E(nc.sbuf_tensor(f"outsb{i}", [128, 512], F32))
                 for i in range(2)]
        ptmp = [E(nc.sbuf_tensor(f"ptmp{i}", [8, 2048], F32)) for i in range(2)]
        out8 = [E(nc.sbuf_tensor("out80", [8, 2048], F32))]
        s_row = None
        bc_sb = E(nc.sbuf_tensor("bc_sb", [128, 1024], F32))
        s_row = Bo[0:1, 1024:2048]
        lnv_sb = None
        ones1 = E(nc.sbuf_tensor("ones1L2", [1, 128], F32))
        ones_s = E(nc.sbuf_tensor("ones_s", [128, 1], BF16))
        sel_sb = E(nc.sbuf_tensor("sel_sb", [8, 2], F32))
        wos = QTs[:, 0:16 * 512]
        borow = out8[0][0:1, :]

        sems = _mk_sems(ctx, nc, [
            "gMisc", "sIn", "sKV", "tL", "nE", "nM", "tPV", "nEvac",
            "vRec", "tBc", "vBc", "gWo", "tO", "vOut", "sO", "tSel",
            "sX2", "sP2",
        ])

        with ExitStack() as c2:
            accT = c2.enter_context(nc.psum_tensor("accT", [128, 1024], F32))
            sums = c2.enter_context(nc.psum_tensor("sums", [128, 1024], F32))
            lg = [c2.enter_context(nc.psum_tensor(f"lg{i}", [128, 1024], F32))
                  for i in range(2)]
            plan = Plan(sems)
            plan.op("gpsimd", lambda e: e.memset(ones1[:, :], 1.0), inc="gMisc")
            plan.op("gpsimd", lambda e: e.memset(ones_s[:, :], 1.0), inc="gMisc")
            gm_all = plan.n("gMisc")
            plan.dma("sync", mask_sb[:, :].rearrange("p (r q) -> p r q", q=128),
                     masks.rearrange("r p q -> p r q"), inc="sIn")
            plan.dma("sync", sel_sb[:, :], sel[:, :], inc="sIn")
            plan.dma("sync", borow, bo.rearrange("(o f) -> o f", o=1),
                     inc="sIn")

            def qt_dma(hh):
                plan.dma("sync", QTs[:, hh * 2048:(hh + 1) * 2048],
                         QT[hh * 128:(hh + 1) * 128, :], inc="sIn")

            kv_guard = {}
            masked_cnt = {}
            state = {"cn": 0, "pend": None}

            def kv_dma(idx):
                h, rs = idx // 8, idx % 4
                pr = idx % 3
                w = [("gMisc", gm_all)] if idx < 3 else \
                    [("tPV", kv_guard[idx - 3])]
                plan.dma("sync", Krs[pr][:, :],
                         KTg[rs * 2048 + h * 128: rs * 2048 + (h + 1) * 128, :],
                         inc="sKV", waits=w)
                plan.dma("sync",
                         Vrs[pr][:, :].rearrange("p (k f) -> p k f", f=128),
                         Vg[rs * 2048:(rs + 1) * 2048,
                            h * 128:(h + 1) * 128].rearrange(
                             "(k s) f -> s k f", s=128),
                         inc="sKV", waits=w)

            def emit_pv(job):
                (h, half, rs, jj, mychunk, q0, cw, pr, first_of_half,
                 last_jj) = job
                hs = half * 1024
                masked = mychunk in masked_cnt
                pieces = []
                p0 = q0 + (128 if masked else 0)
                while p0 < q0 + cw:
                    pb = hs + ((p0 - hs) // 512 + 1) * 512
                    pieces.append((p0, min(pb, q0 + cw) - p0, False))
                    p0 = pb
                if masked:
                    pieces.append((q0, 128, True))
                if not pieces:
                    pieces = [(q0, cw, masked)]
                npc = len(pieces)
                for i, (pq, pw, is_masked) in enumerate(pieces):
                    bank = (pq - hs) // 512
                    w = []
                    if i == 0:
                        if first_of_half:
                            idx = h * 2 + half
                            if idx >= 1:
                                w.append(("nEvac", idx))
                        if not is_masked:
                            w.append(("nE", mychunk + 1))
                    if is_masked:
                        w.append(("nM", masked_cnt[mychunk]))
                    st_pv = (bank not in bank_started[0])
                    bank_started[0].add(bank)
                    plan.op("tensor",
                            lambda e, jj=jj, pq=pq, pw=pw, pr=pr, hs=hs,
                            q0=q0, mychunk=mychunk, st=st_pv: e.matmul(
                                accT[:, pq - hs: pq - hs + pw],
                                Vrs[pr][:, jj * 128:(jj + 1) * 128],
                                pT[mychunk % 4][:, pq - q0: pq - q0 + pw],
                                start=st, stop=False,
                                skip_group_check=True),
                            waits=w)
                    st_sm = (bank not in bank_started[1])
                    bank_started[1].add(bank)
                    plan.op("tensor",
                            lambda e, pq=pq, pw=pw, hs=hs, q0=q0,
                            mychunk=mychunk, st=st_sm: e.matmul(
                                sums[0:1, pq - hs: pq - hs + pw],
                                ones_s[:, :],
                                pT[mychunk % 4][:, pq - q0: pq - q0 + pw],
                                start=st, stop=False,
                                skip_group_check=True),
                            inc="tPV" if i == npc - 1 else None)

            def emit_epi(job):
                h, hs, hidx, vrec = job
                for k in range(2):
                    w = [("vRec", vrec)]
                    plan.op("tensor", lambda e, k=k: e.matmul(
                        sums[:, k * 512:(k + 1) * 512], ones1[:, :],
                        s_row[0:1, k * 512:(k + 1) * 512],
                        start=True, stop=True),
                        waits=w, inc="tBc")
                plan.op("scalar", lambda e: e.activation(
                    Bo[:, 0:1024], sums[:, :], AF.Ln),
                    waits=[("tBc", plan.n("tBc"))])
                plan.op("scalar", lambda e: e.activation(
                    bc_sb[:, :], Bo[:, 0:1024], AF.Exp, scale=-1.0),
                    inc="vBc")
                plan.op("vector", lambda e, h=h, hs=hs: e.tensor_mul(
                    attnT[:, h * 2048 + hs: h * 2048 + hs + 1024],
                    accT[:, :], bc_sb[:, :]),
                    waits=[("vBc", plan.n("vBc"))], inc="nEvac")

            bank_started = [set(), set()]
            qt_dma(0)
            qt_dma(1)
            for i in range(3):
                kv_dma(i)
            for hh in range(2, H):
                qt_dma(hh)
            tL_h = {}
            for h in range(H):
                for half in range(2):
                    hs = half * 1024
                    bank_started[0] = set()
                    bank_started[1] = set()
                    last_jj = {b: min(15, (hs + (b + 1) * 512 - 1) // 128)
                               for b in range(2)}
                    for rs in range(4):
                        it = (h * 2 + half) * 4 + rs
                        pr = it % 3
                        first_of_half = (rs == 0)
                        first_of_window = True
                        for jj in range(16):
                            q0g = max(hs, jj * 128)
                            qend = hs + 1024
                            if q0g >= qend:
                                continue
                            q0 = q0g
                            cw = qend - q0
                            mychunk = state["cn"]
                            for ls in range(0, cw, 512):
                                lw = min(512, cw - ls)
                                w = []
                                if ls == 0:
                                    if first_of_window:
                                        w.append(("sKV", 32 * (it + 1)))
                                    if it == (h * 2) * 4 and first_of_window:
                                        w.append(("sIn", 48 + 16 * (h + 1)))
                                    if mychunk >= 2:
                                        w.append(("nE", mychunk - 1))
                                plan.op("tensor",
                                        lambda e, h=h, jj=jj, q0=q0, ls=ls,
                                        lw=lw, pr=pr, mychunk=mychunk:
                                        e.matmul(
                                            lg[mychunk % 2][:, ls:ls + lw],
                                            Krs[pr][:, jj * 128:(jj + 1) * 128],
                                            QTs[:, h * 2048 + q0 + ls:
                                                h * 2048 + q0 + ls + lw],
                                            start=True, stop=True),
                                        waits=w,
                                        inc="tL" if ls + lw == cw else None)
                            if state["pend"] is not None:
                                emit_pv(state["pend"])
                                state["pend"] = None
                            if state.get("pend_epi") is not None:
                                emit_epi(state["pend_epi"])
                                state["pend_epi"] = None
                            w = [("tL", plan.n("tL"))]
                            if mychunk >= 4:
                                w.append(("tPV", mychunk - 3))
                            plan.op("scalar",
                                    lambda e, mychunk=mychunk, cw=cw:
                                    e.activation(
                                        pT[mychunk % 4][:, 0:cw],
                                        lg[mychunk % 2][:, 0:cw], AF.Exp,
                                        scale=SCALE),
                                    waits=w, inc="nE")
                            first_of_window = False
                            if q0 == jj * 128:
                                plan.op("vector",
                                        lambda e, mychunk=mychunk, rs=rs:
                                        e.tensor_mul(
                                            pT[mychunk % 4][:, 0:128],
                                            pT[mychunk % 4][:, 0:128],
                                            mask_sb[:, rs * 128:
                                                    (rs + 1) * 128]),
                                        waits=[("nE", mychunk + 1)],
                                        inc="nM")
                                masked_cnt[mychunk] = plan.n("nM")
                            state["pend"] = (h, half, rs, jj, mychunk,
                                             q0, cw, pr, first_of_half,
                                             last_jj)
                            first_of_half = False
                            state["cn"] += 1
                        kv_guard[it] = state["cn"]
                        if it + 3 < H * 2 * 4:
                            kv_dma(it + 3)
                    if state["pend"] is not None:
                        emit_pv(state["pend"])
                        state["pend"] = None
                    hidx = h * 2 + half
                    plan.op("vector", lambda e: e.tensor_copy(
                        s_row[:, :], sums[0:1, 0:1024]),
                        waits=[("tPV", plan.n("tPV"))], inc="vRec")
                    state["pend_epi"] = (h, hs, hidx, plan.n("vRec"))
                tL_h[h] = plan.n("tL")
            if state.get("pend_epi") is not None:
                emit_epi(state["pend_epi"])
                state["pend_epi"] = None
            flash_tL = plan.n("tL")
            flash_tPV = plan.n("tPV")
            flash_nE = plan.n("nE")
            plan.emit(nc)

        with ExitStack() as c2:
            psO = [c2.enter_context(nc.psum_tensor(f"psO{i}", [128, 512], F32))
                   for i in range(4)]
            psSel = c2.enter_context(nc.psum_tensor("psSel", [2, 2048], F32))
            plan = Plan(sems)
            gate = [("nEvac", 32), ("tL", flash_tL), ("tPV", flash_tPV),
                    ("nE", flash_nE)]

            for nn in range(4):
                w = [("sIn", plan.n("sIn")), ("gMisc", gm_all)]
                if nn == 0:
                    w += gate
                if nn >= 2:
                    w.append(("vOut", nn - 1))
                plan.op("tensor", lambda e, nn=nn: e.matmul(
                    psO[nn % 4][:, :], ones1[:, :],
                    out8[0][0:1, nn * 512:(nn + 1) * 512],
                    start=True, stop=True),
                    waits=w, inc="tO")
                plan.op("vector", lambda e, nn=nn: e.tensor_copy(
                    Bo[:, nn * 512:(nn + 1) * 512], psO[nn % 4][:, :]),
                    waits=[("tO", nn + 1)], inc="vOut")

            ptoks = []
            for c in range(8):
                w = [] if c < 2 else [("tSel", c - 1)]
                ptoks.append(plan.dma("sync", ptmp[c % 2][:, :],
                                      pall[c * 8:(c + 1) * 8, :],
                                      inc="sP2", waits=w))
            for c in range(8):
                for nn in range(4):
                    w = []
                    if nn == 0:
                        w = [ptoks[c]]
                        if c == 0:
                            w.append(("sIn", 48))
                    plan.op("tensor", lambda e, c=c, nn=nn: e.matmul(
                        psSel[:, nn * 512:(nn + 1) * 512], sel_sb[0:8, :],
                        ptmp[c % 2][0:8, nn * 512:(nn + 1) * 512],
                        start=(c == 0), stop=(c == 7)),
                        waits=w, inc="tSel" if nn == 3 else None)

            wos_sl = [wos, QTs[:, 16 * 512:32 * 512]]
            for f4 in range(4):
                if f4 == 0:
                    w = [("nEvac", 16)]
                elif f4 == 1:
                    w = [("nEvac", 24)]
                else:
                    w = [("tO", 4 + (f4 - 1) * 16)]
                src = wo.rearrange("(cc p) f -> p cc f", p=128)[
                    :, :, f4 * 512:(f4 + 1) * 512]
                cast = plan.dma("gpsimd", wos_sl[f4 % 2][:, :].rearrange(
                    "p (cc f) -> p cc f", f=512), src, inc="gWo", waits=w)
                for tt in range(NT):
                    gl = f4 * 16 + tt
                    gi = plan.n("tO")
                    par = gi % 4
                    for cc in range(16):
                        w = []
                        if cc == 0:
                            w = [cast, ("vOut", max(4, plan.n("vOut") - 3))]
                        plan.op("tensor",
                                lambda e, f4=f4, tt=tt, cc=cc, par=par:
                                e.matmul(
                                    psO[par][:, :],
                                    attnT[:, cc * 2048 + tt * 128:
                                          cc * 2048 + (tt + 1) * 128],
                                    wos_sl[f4 % 2][:, cc * 512:(cc + 1) * 512],
                                    start=(cc == 0), stop=(cc == 15)),
                                waits=w, inc="tO" if cc == 15 else None)
                    dxw = [("vOut", max(4, plan.n("vOut") - 1))] if gl >= 2 else []
                    dx = plan.dma("scalar", xt[gl % 2][:, :],
                                  x[tt * 128:(tt + 1) * 128,
                                    f4 * 512:(f4 + 1) * 512],
                                  inc="sX2", waits=dxw)
                    opar = gl % 2
                    w = [("tO", gi + 1), dx]
                    if gl >= 2:
                        w.append(("sO", plan.n("sO") - 16))
                    plan.op("vector", lambda e, par=par, opar=opar:
                            e.scalar_tensor_tensor(
                                outsb[opar][:, :], psO[par][:, :], 1.0,
                                xt[opar][:, :], op0=ALU.mult, op1=ALU.add),
                            waits=w)
                    plan.op("vector", lambda e, f4=f4, opar=opar:
                            e.tensor_add(outsb[opar][:, :], outsb[opar][:, :],
                                         Bo[:, f4 * 512:(f4 + 1) * 512]))
                    if tt == 0:
                        plan.op("vector", lambda e, f4=f4, opar=opar:
                                e.tensor_add(
                                    outsb[opar][0:2, :], outsb[opar][0:2, :],
                                    psSel[:, f4 * 512:(f4 + 1) * 512]),
                                waits=[("tSel", 8)])
                    plan.op("vector", lambda e: e.engine_nop(), inc="vOut")
                    plan.dma("sync", o[tt * 128:(tt + 1) * 128,
                                      f4 * 512:(f4 + 1) * 512],
                             outsb[opar][:, :], inc="sO",
                             waits=[("vOut", plan.n("vOut"))])
            plan.barrier("sync", [("sO", plan.n("sO"))])
            plan.emit(nc)
    return nc



_NC_CACHE = {}


def _get_nc(which):
    if which not in _NC_CACHE:
        _NC_CACHE[which] = build_l1() if which == "l1" else build_l2()
    return _NC_CACHE[which]


def _build_masks(rc):
    m = np.zeros((4, 128, 128), dtype=bf16)
    s = np.arange(128)[:, None]
    q = np.arange(128)[None, :]
    for rs in range(4):
        off = 1 if rc < rs else 0
        m[rs] = (q >= s + off).astype(bf16)
    return m


def kernel(**inputs):
    xfull = np.asarray(inputs["L0_attn_query"], dtype=np.float32)
    wq = np.asarray(inputs["L0_attn_w_q"], np.float32)
    bq = np.asarray(inputs["L0_attn_b_q"], np.float32)
    wk = np.asarray(inputs["L0_attn_w_k"], np.float32)
    bk = np.asarray(inputs["L0_attn_b_k"], np.float32)
    wv = np.asarray(inputs["L0_attn_w_v"], np.float32)
    bv = np.asarray(inputs["L0_attn_b_v"], np.float32)
    wo = np.asarray(inputs["L0_attn_w_o"], np.float32)
    bo = np.asarray(inputs["L0_attn_b_o"], np.float32)
    w1 = np.asarray(inputs["L0_ffn_w1"], np.float32)
    w3 = np.asarray(inputs["L0_ffn_w3"], np.float32)
    w2 = np.asarray(inputs["L0_ffn_w2"], np.float32)
    wr = np.asarray(inputs["L0_ffn_wr"], np.float32)

    nc1 = _get_nc("l1")
    in1 = []
    xs = []
    for c in range(8):
        b, r = c // 4, c % 4
        xc = np.ascontiguousarray(xfull[b, r::4, :])
        xs.append(xc)
        in1.append(dict(x=xc, wq=wq, bq=bq, wk=wk, bk=bk, wv=wv, bv=bv,
                        w1=w1, w3=w3, w2=w2, wr=wr))
    r1 = run_bass_kernel_spmd(nc1, in1, core_ids=list(range(8)))

    KTs = [np.asarray(r1.results[c]["KT"]) for c in range(8)]
    Vs = [np.asarray(r1.results[c]["V"]) for c in range(8)]
    QTs = [np.asarray(r1.results[c]["QT"]) for c in range(8)]
    ps = [np.asarray(r1.results[c]["p"], np.float32) for c in range(8)]

    pall = np.ascontiguousarray(np.stack(ps, axis=0).reshape(64, D))
    nc2 = _get_nc("l2")
    in2 = []
    for c in range(8):
        b, r = c // 4, c % 4
        KTgc = np.ascontiguousarray(
            np.concatenate(KTs[b * 4:(b + 1) * 4], axis=0))
        Vgc = np.ascontiguousarray(
            np.concatenate(Vs[b * 4:(b + 1) * 4], axis=0))
        selm = np.zeros((8, 2), np.float32)
        if b == 0:
            selm[r, 0] = 1.0
            selm[r + 4, 1] = 1.0
        in2.append(dict(KTg=KTgc, Vg=Vgc, QT=QTs[c], x=xs[c], wo=wo, bo=bo,
                        masks=_build_masks(r), sel=selm, pall=pall))
    r2 = run_bass_kernel_spmd(nc2, in2, core_ids=list(range(8)))

    out = np.empty((2, 8192, D), np.float32)
    for c in range(8):
        b, r = c // 4, c % 4
        out[b, r::4, :] = np.asarray(r2.results[c]["o"], np.float32)
    return out
